# version 01
# baseline (speedup 1.0000x reference)
import math
from collections import defaultdict
from contextlib import ExitStack

import numpy as np
import ml_dtypes

import concourse.bass as bass
import concourse.mybir as mybir
from concourse.bass_utils import run_bass_kernel_spmd

F32 = mybir.dt.float32
BF16 = mybir.dt.bfloat16
AF = mybir.ActivationFunctionType
ALU = mybir.AluOpType
AX = mybir.AxisListType

B, S, D = 2, 8192, 2048
DC = D // 128
HC = 16
NH, NKV, HD = 16, 4, 128
WIN = 1024
NR = 7
EPS = 1e-5
TB = 512
TOWN = 2048
HALO = 1024
TEXT = TOWN + HALO
NTB = TEXT // TB
NTBOWN = TOWN // TB
NCORES = 8
SCL = 1.0 / math.sqrt(HD)
NWB = 8
NEXP = 8
NKT = 12

ENGS = ("sync", "tensor", "vector", "scalar", "gpsimd")


class _FakeInst:
    def then_inc(self, *a, **k):
        return self


_FAKE = _FakeInst()


class _Proxy:
    def __init__(self, real, active):
        self._real = real
        self._active = active

    def __getattr__(self, name):
        if self._active:
            return getattr(self._real, name)
        return lambda *a, **k: _FAKE


def build_nc(dump=(), upto="all"):
    STAGE = {"p1": 1, "a1": 2, "a2": 3, "c": 4, "d": 5, "all": 6}[upto]
    nc = bass.Bass("TRN2", target_bir_lowering=False, debug=False)

    X0 = nc.dram_tensor("x0", [DC, 128, TEXT], F32, kind="ExternalInput").ap()
    W1 = nc.dram_tensor("w1t", [NR, HC, 128, DC * 128], F32, kind="ExternalInput").ap()
    WG = nc.dram_tensor("wgt", [NR, HC, 128, DC * 128], F32, kind="ExternalInput").ap()
    W2 = nc.dram_tensor("w2t", [NR, DC, 128, HC * 128], F32, kind="ExternalInput").ap()
    WS1 = nc.dram_tensor("ws1t", [HC, 128, DC * 128], F32, kind="ExternalInput").ap()
    WSG = nc.dram_tensor("wsgt", [HC, 128, DC * 128], F32, kind="ExternalInput").ap()
    WS2 = nc.dram_tensor("ws2t", [DC, 128, HC * 128], F32, kind="ExternalInput").ap()
    WQ = nc.dram_tensor("wqt", [HC, 128, DC * 128], F32, kind="ExternalInput").ap()
    WK = nc.dram_tensor("wkt", [NKV, 128, DC * 128], F32, kind="ExternalInput").ap()
    WV = nc.dram_tensor("wvt", [NKV, 128, DC * 128], F32, kind="ExternalInput").ap()
    WO = nc.dram_tensor("wot", [DC, 128, HC * 128], F32, kind="ExternalInput").ap()
    GW = nc.dram_tensor("gwt", [128, DC * NR], F32, kind="ExternalInput").ap()
    GB = nc.dram_tensor("gbt", [NR, 1], F32, kind="ExternalInput").ap()
    BIAS = nc.dram_tensor("biast", [1, NR], F32, kind="ExternalInput").ap()
    LM = nc.dram_tensor("lmt", [1, NR * NR], F32, kind="ExternalInput").ap()
    OH = nc.dram_tensor("oht", [NR, NR * 128], F32, kind="ExternalInput").ap()
    COS = nc.dram_tensor("cost", [128, TEXT], F32, kind="ExternalInput").ap()
    SIN = nc.dram_tensor("sint", [128, TEXT], F32, kind="ExternalInput").ap()
    MASK = nc.dram_tensor("maskt", [NTBOWN, 128, NKT * TB], BF16, kind="ExternalInput").ap()
    IDB = nc.dram_tensor("idbt", [128, 128], BF16, kind="ExternalInput").ap()
    IDF = nc.dram_tensor("idft", [128, 128], F32, kind="ExternalInput").ap()

    OUT = nc.dram_tensor("out", [TOWN, D], F32, kind="ExternalOutput").ap()

    def scratch(name, shape, dt):
        kind = "ExternalOutput" if name in dump else "Internal"
        return nc.dram_tensor(name, shape, dt, kind=kind).ap()

    w1b = scratch("w1b", [NR, HC, 128, DC * 128], BF16)
    wgb = scratch("wgb", [NR, HC, 128, DC * 128], BF16)
    w2b = scratch("w2b", [NR, DC, 128, HC * 128], BF16)
    ws1b = scratch("ws1b", [HC, 128, DC * 128], BF16)
    wsgb = scratch("wsgb", [HC, 128, DC * 128], BF16)
    ws2b = scratch("ws2b", [DC, 128, HC * 128], BF16)

    x1f = scratch("x1f", [DC, 128, TEXT], F32)
    xn2b = scratch("xn2b", [DC, 128, TEXT], BF16)
    x2f = scratch("x2f", [DC, 128, TOWN], F32)
    xn3b = scratch("xn3b", [DC, 128, TOWN], BF16)
    qtb = scratch("qtb", [NH, 128, TOWN], BF16)
    atb = scratch("atb", [HC, 128, TOWN], BF16)
    xn1d = scratch("xn1d", [DC, 128, TEXT], BF16) if "xn1d" in dump else None
    smxd = scratch("smxd", [128, NTB * 256], F32) if "smxd" in dump else None
    gtd = scratch("gtd", [128, NTB * 4 * NR], F32) if "smxd" in dump else None
    w7d = scratch("w7d", [NR, TEXT], F32) if "w7d" in dump else None
    wbd = scratch("wbd", [128, TEXT], F32) if "wbd" in dump else None

    CAST_MATS = []
    for e in range(NR):
        CAST_MATS.append((W1[e], w1b[e], HC))
        CAST_MATS.append((WG[e], wgb[e], HC))
        CAST_MATS.append((W2[e], w2b[e], DC))
    CAST_MATS.append((WS1, ws1b, HC))
    CAST_MATS.append((WSG, wsgb, HC))
    CAST_MATS.append((WS2, ws2b, DC))

    big = ExitStack()

    sems = {}
    for name in ["c_sq", "c_dve", "c_act", "c_gp",
                 "f_sq", "f_pe", "f_dve", "f_act", "f_gp",
                 "a_sq", "a_pe", "a_dve", "a_act", "a_gp"]:
        sems[name] = big.enter_context(nc.semaphore(name))

    def sb(name, shape, dt):
        return big.enter_context(nc.sbuf_tensor(name, shape, dt))

    ones_bf = sb("ones_bf", [128, 1], BF16)
    ones_row_f = sb("ones_row_f", [1, 128], F32)
    gb_sb = sb("gb_sb", [NR, 1], F32)
    bias_row = sb("bias_row", [1, NR], F32)
    lm_row = sb("lm_row", [1, NR * NR], F32)
    biases_bc = sb("biases_bc", [128, NR], F32)
    lm_bc = sb("lm_bc", [128, NR * NR], F32)
    oh_sb = sb("oh_sb", [NR, NR * 128], F32)
    gwb_sb = sb("gwb_sb", [128, DC * NR], BF16)
    idb_sb = sb("idb_sb", [128, 128], BF16)
    idf_sb = sb("idf_sb", [128, 128], F32)
    eps_sb = sb("eps_sb", [1, 1], F32)
    qn2_sb = sb("qn2_sb", [1, NH], F32)
    kn2_sb = sb("kn2_sb", [1, NKV], F32)
    negmb_sb = sb("negmb_sb", [128, NH], F32)
    rowtmp = sb("rowtmp", [1, 4 * TB], F32)

    ffn_sb = ExitStack()

    def fsb(name, shape, dt):
        return ffn_sb.enter_context(nc.sbuf_tensor(name, shape, dt))

    wb = [fsb(f"wb{i}", [128, 16 * 128], BF16) for i in range(NWB)]
    xf0 = fsb("xf0", [128, DC * TB], F32)
    xb_sb = fsb("xb_sb", [128, DC * TB], BF16)
    xn_sb = fsb("xn_sb", [128, DC * TB], BF16)
    acc_sb = fsb("acc_sb", [128, DC * TB], F32)
    a_sb = fsb("a_sb", [128, HC * TB], BF16)
    tsil = [fsb(f"tsil{i}", [128, TB], F32) for i in range(2)]
    t2_sb = fsb("t2_sb", [128, TB], F32)
    mB_sb = fsb("mB_sb", [128, TB], F32)
    rB_sb = fsb("rB_sb", [128, TB], F32)
    gt_sb = fsb("gt_sb", [128, 4 * NR], F32)
    w7_sb = fsb("w7_sb", [NR, TB], F32)
    smx_sb = fsb("smx_sb", [128, 256], F32)
    cmp_sb = fsb("cmp_sb", [128, 4 * NR * NR], F32)
    eq_sb = fsb("eq_sb", [128, 4 * NR * NR], F32)
    gwf_sb = fsb("gwf_sb", [128, DC * NR], F32)
    out_sb = [fsb(f"out_sb{i}", [128, D], F32) for i in range(2)]
    cf = [acc_sb[:, i * 2048:(i + 1) * 2048].bitcast(F32) for i in range(4)]

    ffn_ps = ExitStack()

    def fps(name):
        return ffn_ps.enter_context(nc.psum_tensor(name, [128, TB], F32))

    pA = [fps(f"pA{i}") for i in range(2)]
    pB = [fps(f"pB{i}") for i in range(2)]
    pW = [fps(f"pW{i}") for i in range(2)]
    pu = [fps(f"pu{i}") for i in range(2)]
    ffn_ps.close()
    ffn_sb.close()

    at_sb = ExitStack()

    def asb(name, shape, dt):
        return at_sb.enter_context(nc.sbuf_tensor(name, shape, dt))

    kt_sb = asb("kt_sb", [128, NKV * TEXT], BF16)
    vt_sb = asb("vt_sb", [128, NKV * TEXT], BF16)
    axf = asb("axf", [128, DC * TB], F32)
    axb = asb("axb", [128, DC * TB], BF16)
    acf = [asb(f"acf{i}", [128, 16 * 128], F32) for i in range(2)]
    wstr = [asb(f"wstr{i}", [128, 16 * 128], BF16) for i in range(4)]
    csb = [asb(f"csb{i}", [128, TB], F32) for i in range(2)]
    snb = [asb(f"snb{i}", [128, TB], F32) for i in range(2)]
    rop = [asb(f"rop{i}", [128, TB], F32) for i in range(2)]
    kvev = asb("kvev", [128, TB], BF16)
    pt_sb = asb("pt_sb", [128, NKT * TB], BF16)
    mk_sb = asb("mk_sb", [128, NKT * TB], BF16)
    qblk = [asb(f"qblk{i}", [128, TB], BF16) for i in range(2)]
    atev = asb("atev", [128, TB], BF16)
    rrow_sb = asb("rrow_sb", [1, TB], F32)
    rBq_sb = asb("rBq_sb", [128, TB], F32)
    sq_sb = asb("sq_sb", [128, TB], BF16)
    aob = asb("aob", [128, HC * TB], BF16)
    arow = rowtmp

    at_ps = ExitStack()
    pS = [at_ps.enter_context(nc.psum_tensor(f"pS{i}", [128, TB], F32)) for i in range(3)]
    pV = at_ps.enter_context(nc.psum_tensor("pV", [128, TB], F32))
    pden = at_ps.enter_context(nc.psum_tensor("pden", [128, TB], F32))
    pu2 = [at_ps.enter_context(nc.psum_tensor(f"pu2{i}", [128, TB], F32)) for i in range(2)]
    pbf = at_ps.enter_context(nc.psum_tensor("pbf", [128, 1024], BF16))
    at_ps.close()
    at_sb.close()

    def walk(active):
        Scnt = defaultdict(int)
        E = {n: _Proxy(getattr(nc, n), active == n) for n in ENGS}
        sync, tensor, vector, scalar, gpsimd = (E[n] for n in ENGS)
        guards = {}

        def inc(inst, sem, n=1):
            inst.then_inc(sems[sem], n)
            Scnt[sem] += n
            return Scnt[sem]

        def dma(q, sem, out, in_, **kw):
            return inc(E[q].dma_start(out=out, in_=in_, **kw), sem, 16)

        def guard(key, sem, cnt):
            guards[key] = (sem, cnt)

        def wait_guard(eng, key):
            if key in guards:
                s, c = guards[key]
                eng.wait_ge(sems[s], c)

        dma("gpsimd", "c_gp", gb_sb[:, :], GB)
        dma("gpsimd", "c_gp", bias_row[:, :], BIAS)
        dma("gpsimd", "c_gp", lm_row[:, :], LM)
        dma("gpsimd", "c_gp", oh_sb[:, :], OH)
        dma("gpsimd", "c_gp", gwf_sb[:, :], GW)
        dma("gpsimd", "c_gp", idb_sb[:, :], IDB)
        c_small = dma("gpsimd", "c_gp", idf_sb[:, :], IDF)
        vector.wait_ge(sems["c_gp"], c_small)
        vector.memset(ones_bf[:, :], 1.0)
        vector.memset(eps_sb[:, :], EPS)
        vector.memset(ones_row_f[:, :], 1.0)
        vector.memset(qn2_sb[:, :], 0.0)
        vector.memset(kn2_sb[:, :], 0.0)
        cast_gw = inc(vector.tensor_copy(gwb_sb[:, :], gwf_sb[:, :]), "c_dve")
        tensor.wait_ge(sems["c_dve"], cast_gw)
        b2 = inc(tensor.matmul(pu[0][:, 0:NR], ones_row_f[:, :], bias_row[:, :],
                               start=True, stop=True), "f_pe")
        b2 = inc(tensor.matmul(pu[1][:, 0:NR * NR], ones_row_f[:, :], lm_row[:, :],
                               start=True, stop=True), "f_pe")
        scalar.wait_ge(sems["f_pe"], b2)
        scalar.copy(biases_bc[:, :], pu[0][:, 0:NR])
        bc_done = inc(scalar.copy(lm_bc[:, :], pu[1][:, 0:NR * NR]), "c_act")
        guard(("pu", 0), "c_act", bc_done)
        guard(("pu", 1), "c_act", bc_done)

        stripes = []
        for (src, dst, noc) in CAST_MATS:
            for oc in range(noc):
                stripes.append((src[oc], dst[oc]))
        dve_c = {}
        act_c = {}
        st_c = {}
        for i, (saps, daps) in enumerate(stripes):
            if i >= 4:
                sync.wait_ge(sems["c_gp"], st_c[i - 4])
            ldc = dma("sync", "c_sq", cf[i % 4][:, :], saps)
            if i % 2 == 0:
                vector.wait_ge(sems["c_sq"], ldc)
                if i >= 4:
                    vector.wait_ge(sems["c_gp"], st_c[i - 4])
                cc = inc(vector.tensor_copy(wb[i % 4][:, :], cf[i % 4][:, :]), "c_dve")
                dve_c[i] = cc
                gpsimd.wait_ge(sems["c_dve"], cc)
            else:
                scalar.wait_ge(sems["c_sq"], ldc)
                if i >= 4:
                    scalar.wait_ge(sems["c_gp"], st_c[i - 4])
                cc = inc(scalar.copy(wb[i % 4][:, :], cf[i % 4][:, :]), "c_act")
                act_c[i] = cc
                gpsimd.wait_ge(sems["c_act"], cc)
            st_c[i] = dma("gpsimd", "c_gp", daps, wb[i % 4][:, :])
        cast_store_total = Scnt["c_gp"]

        def ffn_phase(ph, ntb):
            stripe_i = [0]
            ring_pe = {}

            def stripe_load(mat_aps, oc):
                i = stripe_i[0]
                stripe_i[0] += 1
                bidx = i % NWB
                if ph == 1 and i < NWB:
                    sync.wait_ge(sems["c_gp"], cast_store_total)
                if i - NWB >= 0:
                    sync.wait_ge(sems["f_pe"], ring_pe[i - NWB])
                c = dma("sync", "f_sq", wb[bidx][:, :], mat_aps[oc])
                return i, bidx, c

            gp_xstore = {}
            last_xn_pe = [0]
            last_xb_pe = [0]
            prev_ep = [0]
            src = X0 if ph == 1 else x2f

            for tb in range(ntb):
                xbuf = xf0
                wait_guard(sync, "xf0")
                xl = 0
                for c0 in range(0, DC, 4):
                    xl = dma("sync", "f_sq",
                             xbuf[:, c0 * TB:(c0 + 4) * TB].rearrange("p (c t) -> p c t", c=4),
                             src[c0:c0 + 4, :, tb * TB:(tb + 1) * TB].rearrange("c p t -> p c t"))

                if ph == 1:
                    scalar.wait_ge(sems["f_sq"], xl)
                    scalar.wait_ge(sems["f_pe"], last_xb_pe[0])
                    for dc in range(DC):
                        ins = scalar.copy(xb_sb[:, dc * TB:(dc + 1) * TB],
                                          xbuf[:, dc * TB:(dc + 1) * TB])
                    xb_done = inc(ins, "f_act")
                    for dc in range(DC):
                        ins = scalar.activation(a_sb[:, dc * TB:(dc + 1) * TB],
                                                xb_sb[:, dc * TB:(dc + 1) * TB], AF.Square)
                    sq_done = inc(ins, "f_act")
                    tensor.wait_ge(sems["f_act"], xb_done)
                    wait_guard(tensor, ("pu", 0))
                    for dc in range(DC):
                        ins = tensor.matmul(pu[0][0:1, :], ones_bf[:, :],
                                            xb_sb[:, dc * TB:(dc + 1) * TB],
                                            start=(dc == 0), stop=(dc == DC - 1))
                    c1 = inc(ins, "f_pe")
                    tensor.wait_ge(sems["f_act"], sq_done)
                    wait_guard(tensor, ("pu", 1))
                    for dc in range(DC):
                        ins = tensor.matmul(pu[1][0:1, :], ones_bf[:, :],
                                            a_sb[:, dc * TB:(dc + 1) * TB],
                                            start=(dc == 0), stop=(dc == DC - 1))
                    c2 = inc(ins, "f_pe")
                    last_xb_pe[0] = c2
                    mrow = rowtmp[0:1, 0:TB]
                    trow = rowtmp[0:1, TB:2 * TB]
                    vrow = rowtmp[0:1, 2 * TB:3 * TB]
                    srow = rowtmp[0:1, 3 * TB:4 * TB]
                    vector.wait_ge(sems["f_pe"], c2)
                    vector.tensor_scalar_mul(mrow, pu[0][0:1, :], 1.0 / D)
                    c_ = inc(vector.tensor_scalar_mul(trow, pu[1][0:1, :], 1.0 / D), "f_dve")
                    vector.wait_ge(sems["f_dve"], c_)
                    c_ = inc(vector.tensor_mul(vrow, mrow, mrow), "f_dve")
                    vector.wait_ge(sems["f_dve"], c_)
                    r1 = inc(vector.tensor_sub(vrow, trow, vrow), "f_dve")
                    scalar.wait_ge(sems["f_dve"], r1)
                    r2 = inc(scalar.activation(srow, vrow, AF.Sqrt, bias=eps_sb[0:1, 0:1]), "f_act")
                    vector.wait_ge(sems["f_act"], r2)
                    r3 = inc(vector.reciprocal(vrow, srow), "f_dve")
                    tensor.wait_ge(sems["f_dve"], r3)
                    c3 = inc(tensor.matmul(pu[0][:, :], ones_row_f[:, :], mrow,
                                           start=True, stop=True), "f_pe")
                    c4 = inc(tensor.matmul(pu[1][:, :], ones_row_f[:, :], vrow,
                                           start=True, stop=True), "f_pe")
                    scalar.wait_ge(sems["f_pe"], c4)
                    scalar.copy(mB_sb[:, :], pu[0][:, :])
                    c5 = inc(scalar.copy(rB_sb[:, :], pu[1][:, :]), "f_act")
                    guard(("pu", 0), "f_act", c5)
                    guard(("pu", 1), "f_act", c5)
                    vector.wait_ge(sems["f_act"], c5)
                    vector.wait_ge(sems["f_pe"], last_xn_pe[0])
                    if tb >= 1:
                        vector.wait_ge(sems["f_gp"], gp_xstore[tb - 1])
                    vector.wait_ge(sems["f_sq"], xl)
                    for dc in range(DC):
                        sl = slice(dc * TB, (dc + 1) * TB)
                        vector.tensor_sub(t2_sb[:, :], xbuf[:, sl], mB_sb[:, :])
                        ins = vector.tensor_mul(xn_sb[:, sl], t2_sb[:, :], rB_sb[:, :])
                    xn_ready = ("f_dve", inc(ins, "f_dve"))
                    if xn1d is not None:
                        gpsimd.wait_ge(sems["f_dve"], xn_ready[1])
                        for c0 in range(0, DC, 4):
                            dma("gpsimd", "f_gp",
                                xn1d[c0:c0 + 4, :, tb * TB:(tb + 1) * TB].rearrange(
                                    "c p t -> p c t"),
                                xn_sb[:, c0 * TB:(c0 + 4) * TB].rearrange(
                                    "p (c t) -> p c t", c=4))
                else:
                    sync.wait_ge(sems["f_pe"], last_xn_pe[0])
                    ld = 0
                    for c0 in range(0, DC, 4):
                        ld = dma("sync", "f_sq",
                                 xn_sb[:, c0 * TB:(c0 + 4) * TB].rearrange(
                                     "p (c t) -> p c t", c=4),
                                 xn3b[c0:c0 + 4, :, tb * TB:(tb + 1) * TB].rearrange(
                                     "c p t -> p c t"))
                    xn_ready = ("f_sq", ld)

                tensor.wait_ge(sems[xn_ready[0]], xn_ready[1])
                if tb >= 1:
                    tensor.wait_ge(sems["f_dve"], prev_ep[0])
                for dc in range(DC):
                    ins = tensor.matmul(pW[0][0:NR, :],
                                        gwb_sb[:, dc * NR:(dc + 1) * NR],
                                        xn_sb[:, dc * TB:(dc + 1) * TB],
                                        start=(dc == 0), stop=(dc == DC - 1))
                g1 = inc(ins, "f_pe")
                vector.wait_ge(sems["f_pe"], g1)
                g2 = inc(vector.tensor_scalar(w7_sb[:, :], pW[0][0:NR, :],
                                              gb_sb[0:NR, 0:1], None, ALU.add), "f_dve")
                tensor.wait_ge(sems["f_dve"], g2)
                for t in range(4):
                    ins = tensor.transpose(pW[1][:, t * NR:(t + 1) * NR],
                                           w7_sb[0:NR, t * 128:(t + 1) * 128],
                                           idf_sb[0:NR, 0:NR])
                g3 = inc(ins, "f_pe")
                scalar.wait_ge(sems["f_pe"], g3)
                g4 = inc(scalar.copy(gt_sb[:, :], pW[1][:, 0:4 * NR]), "f_act")

                gt3 = gt_sb[:, :].rearrange("p (a f) -> p a f", a=4)
                nm = smx_sb[:, 0:4]
                nm3 = nm.rearrange("p (a f) -> p a f", f=1)
                z_ = smx_sb[:, 8:8 + 4 * NR]
                z3 = z_.rearrange("p (a f) -> p a f", a=4)
                pexp = smx_sb[:, 40:40 + 4 * NR]
                pex3 = pexp.rearrange("p (a f) -> p a f", a=4)
                ssum = smx_sb[:, 72:76]
                ss3 = ssum.rearrange("p (a f) -> p a f", f=1)
                prb = smx_sb[:, 80:80 + 4 * NR]
                prb3 = prb.rearrange("p (a f) -> p a f", a=4)
                bsd = smx_sb[:, 112:112 + 4 * NR]
                bsd3 = bsd.rearrange("p (a f) -> p a f", a=4)
                rnk = smx_sb[:, 144:144 + 4 * NR]
                rnk4 = rnk.rearrange("p (a e o) -> p a e o", a=4, o=1)
                sel = smx_sb[:, 176:176 + 4 * NR]
                wtk = smx_sb[:, 208:208 + 4 * NR]
                wt3 = wtk.rearrange("p (a f) -> p a f", a=4)
                wsum = smx_sb[:, 240:244]
                ws3 = wsum.rearrange("p (a f) -> p a f", f=1)
                cmp4 = cmp_sb[:, :].rearrange("p (a e f) -> p a e f", a=4, e=NR)
                eq4 = eq_sb[:, :].rearrange("p (a e f) -> p a e f", a=4, e=NR)
                a0 = bsd3.rearrange("p a (o f) -> p a o f", o=1).broadcast_to([128, 4, NR, NR])
                a1 = bsd3.rearrange("p a (f o) -> p a f o", o=1).broadcast_to([128, 4, NR, NR])

                rs_ = smx_sb[:, 4:8]
                rs3 = rs_.rearrange("p (a f) -> p a f", f=1)
                rws_ = smx_sb[:, 244:248]
                rws3 = rws_.rearrange("p (a f) -> p a f", f=1)

                def vstep(ins):
                    c = inc(ins, "f_dve")
                    vector.wait_ge(sems["f_dve"], c)
                    return c

                vector.wait_ge(sems["f_act"], g4)
                vstep(vector.reduce_max(nm3, gt3, AX.X, negate=True))
                s1 = vstep(vector.tensor_tensor(z3, gt3,
                                                nm3.broadcast_to([128, 4, NR]), ALU.add))
                scalar.wait_ge(sems["f_dve"], s1)
                s2 = inc(scalar.activation(pexp, z_, AF.Exp), "f_act")
                vector.wait_ge(sems["f_act"], s2)
                vstep(vector.reduce_sum(ss3, pex3, AX.X))
                vstep(vector.reciprocal(rs_, ssum))
                vstep(vector.tensor_tensor(prb3, pex3, rs3.broadcast_to([128, 4, NR]),
                                           ALU.mult))
                vstep(vector.tensor_tensor(
                    bsd3, prb3,
                    biases_bc[:, :].rearrange("p (o f) -> p o f", o=1)
                    .broadcast_to([128, 4, NR]), ALU.add))
                vstep(vector.tensor_tensor(cmp4, a0, a1, ALU.is_gt))
                vstep(vector.tensor_tensor(eq4, a0, a1, ALU.is_equal))
                vstep(vector.tensor_tensor(
                    eq4, eq4,
                    lm_bc[:, :].rearrange("p (o e f) -> p o e f", o=1, e=NR)
                    .broadcast_to([128, 4, NR, NR]), ALU.mult))
                vstep(vector.tensor_tensor(cmp4, cmp4, eq4, ALU.add))
                vstep(vector.reduce_sum(rnk4, cmp4, AX.X))
                vstep(vector.tensor_single_scalar(sel, rnk, 3.0, ALU.is_lt))
                vstep(vector.tensor_tensor(wtk, prb, sel, ALU.mult))
                vstep(vector.reduce_sum(ws3, wt3, AX.X))
                vstep(vector.reciprocal(rws_, wsum))
                s3 = vstep(vector.tensor_tensor(wt3, wt3,
                                                rws3.broadcast_to([128, 4, NR]),
                                                ALU.mult))
                if smxd is not None and ph == 1:
                    gpsimd.wait_ge(sems["f_dve"], s3)
                    dma("gpsimd", "f_gp", smxd[:, tb * 256:(tb + 1) * 256], smx_sb[:, :])
                    dma("gpsimd", "f_gp", gtd[:, tb * 4 * NR:(tb + 1) * 4 * NR],
                        gt_sb[:, :])
                tensor.wait_ge(sems["f_dve"], s3)
                for t in range(4):
                    ins = tensor.transpose(pW[0][0:NR, t * 128:(t + 1) * 128],
                                           wtk[:, t * NR:(t + 1) * NR], idf_sb[:, :])
                s4 = inc(ins, "f_pe")
                scalar.wait_ge(sems["f_pe"], s4)
                s5 = inc(scalar.copy(w7_sb[:, :], pW[0][0:NR, :]), "f_act")
                if w7d is not None and ph == 1:
                    gpsimd.wait_ge(sems["f_act"], s5)
                    dma("gpsimd", "f_gp", w7d[:, tb * TB:(tb + 1) * TB], w7_sb[:, :])

                mats = [(ws1b, wsgb, ws2b)] + [(w1b[e], wgb[e], w2b[e]) for e in range(NR)]
                silu_rec = {}
                a_rec = {}
                accrec = {}
                wb_pe = {}
                for e in range(NEXP):
                    m1, mg, m2 = mats[e]
                    if e >= 1:
                        tensor.wait_ge(sems["f_act"], s5)
                        wait_guard(tensor, ("puw", e % 2))
                        wb_pe[e] = inc(tensor.matmul(
                            pu[e % 2][:, :], oh_sb[:, (e - 1) * 128:e * 128],
                            w7_sb[:, :], start=True, stop=True), "f_pe")
                        if wbd is not None and ph == 1 and e == 1:
                            scalar.wait_ge(sems["f_pe"], wb_pe[e])
                            wdcp = inc(scalar.copy(rB_sb[:, :], pu[e % 2][:, :]), "f_act")
                            gpsimd.wait_ge(sems["f_act"], wdcp)
                            dma("gpsimd", "f_gp", wbd[:, tb * TB:(tb + 1) * TB],
                                rB_sb[:, :])
                    for hcb in range(HC):
                        i1, b1_, c1_ = stripe_load(m1, hcb)
                        ig, bg_, cg_ = stripe_load(mg, hcb)
                        ppa = pA[hcb % 2]
                        ppb = pB[hcb % 2]
                        tensor.wait_ge(sems["f_sq"], c1_)
                        wait_guard(tensor, ("pA", hcb % 2))
                        for dc in range(DC):
                            ins = tensor.matmul(ppa[:, :], wb[b1_][:, dc * 128:(dc + 1) * 128],
                                                xn_sb[:, dc * TB:(dc + 1) * TB],
                                                start=(dc == 0), stop=(dc == DC - 1))
                        mm1 = inc(ins, "f_pe")
                        ring_pe[i1] = mm1
                        tensor.wait_ge(sems["f_sq"], cg_)
                        wait_guard(tensor, ("pB", hcb % 2))
                        for dc in range(DC):
                            ins = tensor.matmul(ppb[:, :], wb[bg_][:, dc * 128:(dc + 1) * 128],
                                                xn_sb[:, dc * TB:(dc + 1) * TB],
                                                start=(dc == 0), stop=(dc == DC - 1))
                        mmg = inc(ins, "f_pe")
                        ring_pe[ig] = mmg
                        last_xn_pe[0] = mmg
                        scalar.wait_ge(sems["f_pe"], mm1)
                        wait_guard(scalar, ("tsil", hcb % 2))
                        srec = inc(scalar.activation(tsil[hcb % 2][:, :], ppa[:, :], AF.Silu),
                                   "f_act")
                        silu_rec[(e, hcb)] = srec
                        guard(("pA", hcb % 2), "f_act", srec)
                        vector.wait_ge(sems["f_act"], srec)
                        vector.wait_ge(sems["f_pe"], mmg)
                        asl = a_sb[:, hcb * TB:(hcb + 1) * TB]
                        if e == 0:
                            ins = vector.tensor_tensor(asl, tsil[hcb % 2][:, :], ppb[:, :],
                                                       ALU.mult)
                        else:
                            vector.wait_ge(sems["f_pe"], wb_pe[e])
                            vector.tensor_tensor(t2_sb[:, :], tsil[hcb % 2][:, :],
                                                 pu[e % 2][:, :], ALU.mult)
                            ins = vector.tensor_tensor(asl, t2_sb[:, :], ppb[:, :], ALU.mult)
                        arec = inc(ins, "f_dve")
                        a_rec[(e, hcb)] = arec
                        guard(("pB", hcb % 2), "f_dve", arec)
                        guard(("tsil", hcb % 2), "f_dve", arec)
                        if hcb == HC - 1:
                            guard(("puw", e % 2), "f_dve", arec)
                    for dc in range(DC):
                        i2, b2_, c2_ = stripe_load(m2, dc)
                        ppw = pW[dc % 2]
                        tensor.wait_ge(sems["f_sq"], c2_)
                        if dc == 0:
                            tensor.wait_ge(sems["f_dve"], a_rec[(e, HC - 1)])
                        wait_guard(tensor, ("pW", dc % 2))
                        for hcb in range(HC):
                            ins = tensor.matmul(ppw[:, :], wb[b2_][:, hcb * 128:(hcb + 1) * 128],
                                                a_sb[:, hcb * TB:(hcb + 1) * TB],
                                                start=(hcb == 0), stop=(hcb == HC - 1))
                        mmw = inc(ins, "f_pe")
                        ring_pe[i2] = mmw
                        vector.wait_ge(sems["f_pe"], mmw)
                        dsl = acc_sb[:, dc * TB:(dc + 1) * TB]
                        if e == 0:
                            ins = vector.tensor_copy(dsl, ppw[:, :])
                        else:
                            ins = vector.tensor_add(dsl, dsl, ppw[:, :])
                        acr = inc(ins, "f_dve")
                        accrec[(e, dc)] = acr
                        guard(("pW", dc % 2), "f_dve", acr)

                ep_ins = None
                for dc in range(DC):
                    sl = slice(dc * TB, (dc + 1) * TB)
                    vector.wait_ge(sems["f_dve"], accrec[(NEXP - 1, dc)])
                    ep_ins = vector.scalar_tensor_tensor(xbuf[:, sl], acc_sb[:, sl], 0.5,
                                                         xbuf[:, sl], ALU.mult, ALU.add)
                ep_all = inc(ep_ins, "f_dve")
                prev_ep[0] = ep_all
                if ph == 1:
                    gpsimd.wait_ge(sems["f_dve"], ep_all)
                    for c0 in range(0, DC, 4):
                        dma("gpsimd", "f_gp",
                            x1f[c0:c0 + 4, :, tb * TB:(tb + 1) * TB].rearrange(
                                "c p t -> p c t"),
                            xbuf[:, c0 * TB:(c0 + 4) * TB].rearrange("p (c t) -> p c t", c=4))
                    scalar.wait_ge(sems["f_dve"], ep_all)
                    scalar.wait_ge(sems["f_pe"], last_xb_pe[0])
                    for dc in range(DC):
                        ins = scalar.copy(xb_sb[:, dc * TB:(dc + 1) * TB],
                                          xbuf[:, dc * TB:(dc + 1) * TB])
                    xb1_done = inc(ins, "f_act")
                    for dc in range(DC):
                        ins = scalar.activation(a_sb[:, dc * TB:(dc + 1) * TB],
                                                xb_sb[:, dc * TB:(dc + 1) * TB], AF.Square)
                    sq1_done = inc(ins, "f_act")
                    tensor.wait_ge(sems["f_act"], xb1_done)
                    wait_guard(tensor, ("pu", 0))
                    for dc in range(DC):
                        ins = tensor.matmul(pu[0][0:1, :], ones_bf[:, :],
                                            xb_sb[:, dc * TB:(dc + 1) * TB],
                                            start=(dc == 0), stop=(dc == DC - 1))
                    d1 = inc(ins, "f_pe")
                    tensor.wait_ge(sems["f_act"], sq1_done)
                    wait_guard(tensor, ("pu", 1))
                    for dc in range(DC):
                        ins = tensor.matmul(pu[1][0:1, :], ones_bf[:, :],
                                            a_sb[:, dc * TB:(dc + 1) * TB],
                                            start=(dc == 0), stop=(dc == DC - 1))
                    d2 = inc(ins, "f_pe")
                    last_xb_pe[0] = d2
                    mrow = rowtmp[0:1, 0:TB]
                    rrow = rowtmp[0:1, TB:2 * TB]
                    trow = rowtmp[0:1, TB:2 * TB]
                    vrow = rowtmp[0:1, 2 * TB:3 * TB]
                    srow = rowtmp[0:1, 3 * TB:4 * TB]
                    vector.wait_ge(sems["f_pe"], d2)
                    vector.tensor_scalar_mul(mrow, pu[0][0:1, :], 1.0 / D)
                    c_ = inc(vector.tensor_scalar_mul(trow, pu[1][0:1, :], 1.0 / D), "f_dve")
                    vector.wait_ge(sems["f_dve"], c_)
                    c_ = inc(vector.tensor_mul(vrow, mrow, mrow), "f_dve")
                    vector.wait_ge(sems["f_dve"], c_)
                    e1 = inc(vector.tensor_sub(vrow, trow, vrow), "f_dve")
                    scalar.wait_ge(sems["f_dve"], e1)
                    e2 = inc(scalar.activation(srow, vrow, AF.Sqrt, bias=eps_sb[0:1, 0:1]), "f_act")
                    vector.wait_ge(sems["f_act"], e2)
                    e3 = inc(vector.reciprocal(rrow, srow), "f_dve")
                    tensor.wait_ge(sems["f_dve"], e3)
                    e4 = inc(tensor.matmul(pu[0][:, :], ones_row_f[:, :], mrow,
                                           start=True, stop=True), "f_pe")
                    e5 = inc(tensor.matmul(pu[1][:, :], ones_row_f[:, :], rrow,
                                           start=True, stop=True), "f_pe")
                    scalar.wait_ge(sems["f_pe"], e5)
                    scalar.copy(mB_sb[:, :], pu[0][:, :])
                    e6 = inc(scalar.copy(rB_sb[:, :], pu[1][:, :]), "f_act")
                    guard(("pu", 0), "f_act", e6)
                    guard(("pu", 1), "f_act", e6)
                    vector.wait_ge(sems["f_act"], e6)
                    vector.wait_ge(sems["f_pe"], last_xn_pe[0])
                    for dc in range(DC):
                        sl = slice(dc * TB, (dc + 1) * TB)
                        vector.tensor_sub(t2_sb[:, :], xbuf[:, sl], mB_sb[:, :])
                        ins = vector.tensor_mul(xn_sb[:, sl], t2_sb[:, :], rB_sb[:, :])
                    xn2_done = inc(ins, "f_dve")
                    last_xn_pe[0] = 0
                    gpsimd.wait_ge(sems["f_dve"], xn2_done)
                    st = 0
                    for c0 in range(0, DC, 4):
                        st = dma("gpsimd", "f_gp",
                                 xn2b[c0:c0 + 4, :, tb * TB:(tb + 1) * TB].rearrange(
                                     "c p t -> p c t"),
                                 xn_sb[:, c0 * TB:(c0 + 4) * TB].rearrange(
                                     "p (c t) -> p c t", c=4))
                    gp_xstore[tb] = st
                    guard("xf0", "f_gp", st)
                else:
                    tensor.wait_ge(sems["f_dve"], ep_all)
                    st = 0
                    tt = 0
                    for t in range(4):
                        ob = out_sb[t % 2]
                        wait_guard(scalar, ("out", t % 2))
                        for dc in range(DC):
                            pp = pu[dc % 2]
                            wait_guard(tensor, ("pu", dc % 2))
                            ins = tensor.transpose(
                                pp[:, (dc % 4) * 128:(dc % 4 + 1) * 128],
                                xbuf[:, dc * TB + t * 128:dc * TB + (t + 1) * 128],
                                idf_sb[:, :])
                            tt = inc(ins, "f_pe")
                            scalar.wait_ge(sems["f_pe"], tt)
                            ins2 = scalar.copy(
                                ob[:, dc * 128:(dc + 1) * 128],
                                pp[:, (dc % 4) * 128:(dc % 4 + 1) * 128])
                            td = inc(ins2, "f_act")
                            guard(("pu", dc % 2), "f_act", td)
                        gpsimd.wait_ge(sems["f_act"], td)
                        st = dma("gpsimd", "f_gp",
                                 OUT[(tb * 4 + t) * 128:(tb * 4 + t + 1) * 128, :],
                                 ob[:, :])
                        guard(("out", t % 2), "f_gp", st)
                    gp_xstore[tb] = st
                    guard("xf0", "f_pe", tt)
            return gp_xstore

        ffn_phase(1, NTB)
        p1_gp_total = Scnt["f_gp"]

        if STAGE <= 1:
            gpsimd.wait_ge(sems["f_gp"], Scnt["f_gp"])
            return Scnt

        sync.wait_ge(sems["f_gp"], p1_gp_total)

        def rope_apply(pp, dstap, cosap, sinap, rec_sem):
            vector.tensor_scalar_mul(rop[0][0:64, :], pp[64:128, :], -1.0)
            vector.tensor_copy(rop[0][64:128, :], pp[0:64, :])
            vector.tensor_tensor(rop[1][:, :], pp[:, :], cosap, ALU.mult)
            vector.tensor_tensor(rop[0][:, :], rop[0][:, :], sinap, ALU.mult)
            ins = vector.tensor_tensor(dstap, rop[1][:, :], rop[0][:, :], ALU.add)
            return inc(ins, rec_sem)

        for tb in range(NTB):
            csl = slice(tb * TB, (tb + 1) * TB)
            wait_guard(sync, ("csb", tb % 2))
            dma("sync", "a_sq", csb[tb % 2][:, :], COS[:, csl])
            cl = dma("sync", "a_sq", snb[tb % 2][:, :], SIN[:, csl])
            wait_guard(sync, "axb")
            xl = 0
            for c0 in range(0, DC, 4):
                xl = dma("sync", "a_sq",
                         axb[:, c0 * TB:(c0 + 4) * TB].rearrange("p (c t) -> p c t", c=4),
                         xn2b[c0:c0 + 4, :, tb * TB:(tb + 1) * TB].rearrange("c p t -> p c t"))
            for g in range(2 * NKV):
                srcw = WK[g] if g < NKV else WV[g - NKV]
                wait_guard(sync, ("acf", g % 2))
                c = dma("sync", "a_sq", acf[g % 2][:, :], srcw)
                vector.wait_ge(sems["a_sq"], c)
                wait_guard(vector, ("wstr", g % 4))
                wcast = inc(vector.tensor_copy(wstr[g % 4][:, :], acf[g % 2][:, :]),
                            "a_dve")
                guard(("acf", g % 2), "a_dve", wcast)
                pp = pS[g % 2]
                tensor.wait_ge(sems["a_sq"], xl)
                tensor.wait_ge(sems["a_dve"], wcast)
                wait_guard(tensor, ("pS", g % 2))
                for dc in range(DC):
                    ins = tensor.matmul(
                        pp[:, :], wstr[g % 4][:, dc * 128:(dc + 1) * 128],
                        axb[:, dc * TB:(dc + 1) * TB],
                        start=(dc == 0), stop=(dc == DC - 1))
                mmkv = inc(ins, "a_pe")
                guard("axb", "a_pe", mmkv)
                guard(("wstr", g % 4), "a_pe", mmkv)
                if g < NKV:
                    vector.wait_ge(sems["a_pe"], mmkv)
                    vector.wait_ge(sems["a_sq"], cl)
                    kdone = rope_apply(
                        pp, kt_sb[:, g * TEXT + tb * TB:g * TEXT + (tb + 1) * TB],
                        csb[tb % 2][:, :], snb[tb % 2][:, :], "a_dve")
                    guard(("pS", g % 2), "a_dve", kdone)
                    if g == NKV - 1:
                        guard(("csb", tb % 2), "a_dve", kdone)
                    scalar.wait_ge(sems["a_dve"], kdone)
                    wait_guard(scalar, "sq_sb")
                    ksq = inc(scalar.activation(
                        sq_sb[:, :],
                        kt_sb[:, g * TEXT + tb * TB:g * TEXT + (tb + 1) * TB],
                        AF.Square), "a_act")
                    tensor.wait_ge(sems["a_act"], ksq)
                    wait_guard(tensor, "pden")
                    nk1 = inc(tensor.matmul(pden[0:1, :], ones_bf[:, :], sq_sb[:, :],
                                            start=True, stop=True), "a_pe")
                    guard("sq_sb", "a_pe", nk1)
                    vector.wait_ge(sems["a_pe"], nk1)
                    c_ = inc(vector.reduce_max(arow[0:1, 0:1], pden[0:1, :], AX.X),
                             "a_dve")
                    vector.wait_ge(sems["a_dve"], c_)
                    nupd = inc(vector.tensor_max(kn2_sb[0:1, g:g + 1],
                                                 kn2_sb[0:1, g:g + 1],
                                                 arow[0:1, 0:1]), "a_dve")
                    vector.wait_ge(sems["a_dve"], nupd)
                    guard("pden", "a_dve", nupd)
                else:
                    gg = g - NKV
                    scalar.wait_ge(sems["a_pe"], mmkv)
                    wait_guard(scalar, "kvev")
                    vc = inc(scalar.copy(kvev[:, :], pp[:, :]), "a_act")
                    guard(("pS", g % 2), "a_act", vc)
                    tensor.wait_ge(sems["a_act"], vc)
                    wait_guard(tensor, "pbf")
                    for t in range(4):
                        ins = tensor.transpose(pbf[:, t * 128:(t + 1) * 128],
                                               kvev[:, t * 128:(t + 1) * 128],
                                               idb_sb[:, :])
                    vt_mm = inc(ins, "a_pe")
                    guard("kvev", "a_pe", vt_mm)
                    scalar.wait_ge(sems["a_pe"], vt_mm)
                    ins = scalar.copy(
                        vt_sb[:, gg * TEXT + tb * TB:gg * TEXT + (tb + 1) * TB],
                        pbf[:, 0:TB])
                    vtc = inc(ins, "a_act")
                    guard("pbf", "a_act", vtc)

        if STAGE <= 2:
            gpsimd.wait_ge(sems["a_gp"], Scnt["a_gp"])
            gpsimd.wait_ge(sems["a_dve"], Scnt["a_dve"])
            gpsimd.wait_ge(sems["a_act"], Scnt["a_act"])
            return Scnt

        qn_last = 0
        for tb in range(NTBOWN):
            tbe = tb + 2
            csl = slice(tbe * TB, (tbe + 1) * TB)
            wait_guard(sync, ("csb", tbe % 2))
            dma("sync", "a_sq", csb[tbe % 2][:, :], COS[:, csl])
            cl = dma("sync", "a_sq", snb[tbe % 2][:, :], SIN[:, csl])
            wait_guard(sync, "axb")
            xl = 0
            for c0 in range(0, DC, 4):
                xl = dma("sync", "a_sq",
                         axb[:, c0 * TB:(c0 + 4) * TB].rearrange("p (c t) -> p c t", c=4),
                         xn2b[c0:c0 + 4, :, tbe * TB:(tbe + 1) * TB].rearrange(
                             "c p t -> p c t"))
            for h in range(NH):
                wait_guard(sync, ("acf", h % 2))
                c = dma("sync", "a_sq", acf[h % 2][:, :], WQ[h])
                scalar.wait_ge(sems["a_sq"], c)
                wait_guard(scalar, ("wstr", h % 4))
                wcast = inc(scalar.copy(wstr[h % 4][:, :], acf[h % 2][:, :]), "a_act")
                guard(("acf", h % 2), "a_act", wcast)
                pp = pS[h % 2]
                tensor.wait_ge(sems["a_sq"], xl)
                tensor.wait_ge(sems["a_act"], wcast)
                wait_guard(tensor, ("pS", h % 2))
                for dc in range(DC):
                    ins = tensor.matmul(
                        pp[:, :], wstr[h % 4][:, dc * 128:(dc + 1) * 128],
                        axb[:, dc * TB:(dc + 1) * TB],
                        start=(dc == 0), stop=(dc == DC - 1))
                mmq = inc(ins, "a_pe")
                guard("axb", "a_pe", mmq)
                guard(("wstr", h % 4), "a_pe", mmq)
                vector.wait_ge(sems["a_pe"], mmq)
                vector.wait_ge(sems["a_sq"], cl)
                wait_guard(vector, ("qblk", h % 2))
                wait_guard(vector, ("qblk2", h % 2))
                qdone = rope_apply(pp, qblk[h % 2][:, :], csb[tbe % 2][:, :],
                                   snb[tbe % 2][:, :], "a_dve")
                guard(("pS", h % 2), "a_dve", qdone)
                if h == NH - 1:
                    guard(("csb", tbe % 2), "a_dve", qdone)
                gpsimd.wait_ge(sems["a_dve"], qdone)
                qst = dma("gpsimd", "a_gp", qtb[h, :, tb * TB:(tb + 1) * TB],
                          qblk[h % 2][:, :])
                guard(("qblk", h % 2), "a_gp", qst)
                scalar.wait_ge(sems["a_dve"], qdone)
                wait_guard(scalar, "sq_sb")
                qsq = inc(scalar.activation(sq_sb[:, :], qblk[h % 2][:, :], AF.Square),
                          "a_act")
                guard(("qblk2", h % 2), "a_act", qsq)
                tensor.wait_ge(sems["a_act"], qsq)
                wait_guard(tensor, "pden")
                nq1 = inc(tensor.matmul(pden[0:1, :], ones_bf[:, :], sq_sb[:, :],
                                        start=True, stop=True), "a_pe")
                guard("sq_sb", "a_pe", nq1)
                vector.wait_ge(sems["a_pe"], nq1)
                c_ = inc(vector.reduce_max(arow[0:1, 0:1], pden[0:1, :], AX.X), "a_dve")
                vector.wait_ge(sems["a_dve"], c_)
                qn_last = inc(vector.tensor_max(qn2_sb[0:1, h:h + 1],
                                                qn2_sb[0:1, h:h + 1],
                                                arow[0:1, 0:1]), "a_dve")
                vector.wait_ge(sems["a_dve"], qn_last)
                guard("pden", "a_dve", qn_last)
        a2_store_total = Scnt["a_gp"]

        scalar.wait_ge(sems["a_dve"], qn_last)
        inc(scalar.activation(arow[0:1, 0:NH], qn2_sb[0:1, :], AF.Sqrt), "a_act")
        n2 = inc(scalar.activation(arow[0:1, 32:32 + NKV], kn2_sb[0:1, :], AF.Sqrt),
                 "a_act")
        vector.wait_ge(sems["a_act"], n2)
        c_ = inc(vector.tensor_tensor(
            arow[0:1, 64:64 + NH].rearrange("o (g r) -> o g r", g=NKV),
            arow[0:1, 0:NH].rearrange("o (g r) -> o g r", g=NKV),
            arow[0:1, 32:32 + NKV].rearrange("o (g r) -> o g r", r=1)
            .broadcast_to([1, NKV, NKV]), ALU.mult), "a_dve")
        vector.wait_ge(sems["a_dve"], c_)
        n3 = inc(vector.tensor_scalar_mul(arow[0:1, 64:64 + NH],
                                          arow[0:1, 64:64 + NH], -SCL), "a_dve")
        tensor.wait_ge(sems["a_dve"], n3)
        wait_guard(tensor, "pden")
        n4 = inc(tensor.matmul(pden[:, 0:NH], ones_row_f[:, :], arow[0:1, 64:64 + NH],
                               start=True, stop=True), "a_pe")
        scalar.wait_ge(sems["a_pe"], n4)
        n5 = inc(scalar.copy(negmb_sb[:, :], pden[:, 0:NH]), "a_act")
        guard("pden", "a_act", n5)

        if STAGE <= 3:
            gpsimd.wait_ge(sems["a_gp"], Scnt["a_gp"])
            gpsimd.wait_ge(sems["a_act"], n5)
            return Scnt

        sync.wait_ge(sems["a_gp"], a2_store_total)
        sync.wait_ge(sems["a_act"], n5)
        for qb in range(NTBOWN):
            wait_guard(sync, "mk")
            ml = dma("sync", "a_sq", mk_sb[:, :], MASK[qb, :, :])
            for h in range(NH):
                g = h // 4
                wait_guard(sync, ("qblk", h % 2))
                wait_guard(sync, ("qblk2", h % 2))
                ql = dma("sync", "a_sq", qblk[h % 2][:, :],
                         qtb[h, :, qb * TB:(qb + 1) * TB])
                guard(("qblk2", h % 2), "a_sq", 0)
                guards.pop(("qblk2", h % 2), None)
                msk_last = 0
                for l in range(NKT):
                    pp = pS[l % 3]
                    ext_t = qb * 4 + l
                    tensor.wait_ge(sems["a_sq"], ql)
                    if qb == 0 and h == 0 and l == 0:
                        tensor.wait_ge(sems["a_act"], n5)
                    wait_guard(tensor, ("pS", l % 3))
                    ins = tensor.matmul(
                        pp[:, :], kt_sb[:, g * TEXT + ext_t * 128:g * TEXT + (ext_t + 1) * 128],
                        qblk[h % 2][:, :], start=True, stop=True)
                    mms = inc(ins, "a_pe")
                    if l == NKT - 1:
                        guard(("qblk", h % 2), "a_pe", mms)
                    scalar.wait_ge(sems["a_pe"], mms)
                    wait_guard(scalar, ("pt", l))
                    ins = scalar.activation(pt_sb[:, l * TB:(l + 1) * TB], pp[:, :], AF.Exp,
                                            bias=negmb_sb[:, h:h + 1], scale=SCL)
                    erec = inc(ins, "a_act")
                    guard(("pS", l % 3), "a_act", erec)
                    vector.wait_ge(sems["a_act"], erec)
                    vector.wait_ge(sems["a_sq"], ml)
                    ins = vector.tensor_tensor(pt_sb[:, l * TB:(l + 1) * TB],
                                               pt_sb[:, l * TB:(l + 1) * TB],
                                               mk_sb[:, l * TB:(l + 1) * TB],
                                               ALU.mult)
                    msk_last = inc(ins, "a_dve")
                guard("mk", "a_dve", msk_last)
                tensor.wait_ge(sems["a_dve"], msk_last)
                wait_guard(tensor, "pden")
                for l in range(NKT):
                    ins = tensor.matmul(pden[0:1, :], ones_bf[:, :],
                                        pt_sb[:, l * TB:(l + 1) * TB],
                                        start=(l == 0), stop=(l == NKT - 1))
                den_mm = inc(ins, "a_pe")
                wait_guard(tensor, "pV")
                for l in range(NKT):
                    ext_t = qb * 4 + l
                    ins = tensor.matmul(
                        pV[:, :], vt_sb[:, g * TEXT + ext_t * 128:g * TEXT + (ext_t + 1) * 128],
                        pt_sb[:, l * TB:(l + 1) * TB],
                        start=(l == 0), stop=(l == NKT - 1))
                pv_mm = inc(ins, "a_pe")
                for l in range(NKT):
                    guard(("pt", l), "a_pe", pv_mm)
                vector.wait_ge(sems["a_pe"], den_mm)
                rc = inc(vector.reciprocal(rrow_sb[:, :], pden[0:1, :]), "a_dve")
                guard("pden", "a_dve", rc)
                tensor.wait_ge(sems["a_dve"], rc)
                wait_guard(tensor, ("pu2", 0))
                rb1 = inc(tensor.matmul(pu2[0][:, :], ones_row_f[:, :], rrow_sb[:, :],
                                        start=True, stop=True), "a_pe")
                scalar.wait_ge(sems["a_pe"], rb1)
                rb2 = inc(scalar.copy(rBq_sb[:, :], pu2[0][:, :]), "a_act")
                guard(("pu2", 0), "a_act", rb2)
                vector.wait_ge(sems["a_act"], rb2)
                vector.wait_ge(sems["a_pe"], pv_mm)
                wait_guard(vector, "atev")
                ins = vector.tensor_tensor(atev[:, :], pV[:, :], rBq_sb[:, :], ALU.mult)
                at_done = inc(ins, "a_dve")
                guard("pV", "a_dve", at_done)
                gpsimd.wait_ge(sems["a_dve"], at_done)
                ast = dma("gpsimd", "a_gp", atb[h, :, qb * TB:(qb + 1) * TB], atev[:, :])
                guard("atev", "a_gp", ast)
        atb_store_total = Scnt["a_gp"]

        if STAGE <= 4:
            gpsimd.wait_ge(sems["a_gp"], Scnt["a_gp"])
            return Scnt

        sync.wait_ge(sems["a_gp"], atb_store_total)
        for tb in range(NTBOWN):
            wait_guard(sync, "aob")
            xl = 0
            for c0 in range(0, DC, 4):
                xl = dma("sync", "a_sq",
                         aob[:, c0 * TB:(c0 + 4) * TB].rearrange("p (c t) -> p c t", c=4),
                         atb[c0:c0 + 4, :, tb * TB:(tb + 1) * TB].rearrange(
                             "c p t -> p c t"))
            wait_guard(sync, "axf")
            x1l = 0
            for c0 in range(0, DC, 4):
                x1l = dma("sync", "a_sq",
                          axf[:, c0 * TB:(c0 + 4) * TB].rearrange("p (c t) -> p c t", c=4),
                          x1f[c0:c0 + 4, :, (tb + 2) * TB:(tb + 3) * TB].rearrange(
                              "c p t -> p c t"))
            ev_rec = {}
            d_xn_guard = ("axb_d",)
            for dc in range(DC):
                wait_guard(sync, ("acf", dc % 2))
                c = dma("sync", "a_sq", acf[dc % 2][:, :], WO[dc])
                scalar.wait_ge(sems["a_sq"], c)
                wait_guard(scalar, ("wstr", dc % 4))
                wcast = inc(scalar.copy(wstr[dc % 4][:, :], acf[dc % 2][:, :]), "a_act")
                guard(("acf", dc % 2), "a_act", wcast)
                pp = pu2[dc % 2]
                tensor.wait_ge(sems["a_sq"], xl)
                tensor.wait_ge(sems["a_act"], wcast)
                wait_guard(tensor, ("pu2", dc % 2))
                for hcb in range(HC):
                    ins = tensor.matmul(
                        pp[:, :], wstr[dc % 4][:, hcb * 128:(hcb + 1) * 128],
                        aob[:, hcb * TB:(hcb + 1) * TB],
                        start=(hcb == 0), stop=(hcb == HC - 1))
                mmo = inc(ins, "a_pe")
                guard(("wstr", dc % 4), "a_pe", mmo)
                vector.wait_ge(sems["a_pe"], mmo)
                vector.wait_ge(sems["a_sq"], x1l)
                sl = slice(dc * TB, (dc + 1) * TB)
                ins = vector.tensor_add(axf[:, sl], axf[:, sl], pp[:, :])
                er = inc(ins, "a_dve")
                ev_rec[dc] = er
                guard(("pu2", dc % 2), "a_dve", er)
            x2_all = ev_rec[DC - 1]
            gpsimd.wait_ge(sems["a_dve"], x2_all)
            stx2 = 0
            for c0 in range(0, DC, 4):
                stx2 = dma("gpsimd", "a_gp",
                           x2f[c0:c0 + 4, :, tb * TB:(tb + 1) * TB].rearrange(
                               "c p t -> p c t"),
                           axf[:, c0 * TB:(c0 + 4) * TB].rearrange("p (c t) -> p c t", c=4))
            scalar.wait_ge(sems["a_dve"], x2_all)
            wait_guard(scalar, "axb")
            for dc in range(DC):
                ins = scalar.copy(axb[:, dc * TB:(dc + 1) * TB],
                                  axf[:, dc * TB:(dc + 1) * TB])
            xb3 = inc(ins, "a_act")
            for dc in range(DC):
                ins = scalar.activation(aob[:, dc * TB:(dc + 1) * TB],
                                        axb[:, dc * TB:(dc + 1) * TB], AF.Square)
            sq3 = inc(ins, "a_act")
            tensor.wait_ge(sems["a_act"], xb3)
            wait_guard(tensor, "pden")
            for dc in range(DC):
                ins = tensor.matmul(pden[0:1, :], ones_bf[:, :],
                                    axb[:, dc * TB:(dc + 1) * TB],
                                    start=(dc == 0), stop=(dc == DC - 1))
            d1 = inc(ins, "a_pe")
            tensor.wait_ge(sems["a_act"], sq3)
            wait_guard(tensor, ("pu2", 0))
            for dc in range(DC):
                ins = tensor.matmul(pu2[0][0:1, :], ones_bf[:, :],
                                    aob[:, dc * TB:(dc + 1) * TB],
                                    start=(dc == 0), stop=(dc == DC - 1))
            d2 = inc(ins, "a_pe")
            guard("aob", "a_pe", d2)
            mrow = arow[0:1, 0:TB]
            rrow = arow[0:1, TB:2 * TB]
            vrow = arow[0:1, 2 * TB:3 * TB]
            srow = arow[0:1, 3 * TB:4 * TB]
            vector.wait_ge(sems["a_pe"], d2)
            vector.tensor_scalar_mul(mrow, pden[0:1, :], 1.0 / D)
            c_ = inc(vector.tensor_scalar_mul(rrow, pu2[0][0:1, :], 1.0 / D), "a_dve")
            vector.wait_ge(sems["a_dve"], c_)
            c_ = inc(vector.tensor_mul(vrow, mrow, mrow), "a_dve")
            vector.wait_ge(sems["a_dve"], c_)
            e1 = inc(vector.tensor_sub(vrow, rrow, vrow), "a_dve")
            scalar.wait_ge(sems["a_dve"], e1)
            e2 = inc(scalar.activation(srow, vrow, AF.Sqrt, bias=eps_sb[0:1, 0:1]), "a_act")
            vector.wait_ge(sems["a_act"], e2)
            e3 = inc(vector.reciprocal(rrow, srow), "a_dve")
            tensor.wait_ge(sems["a_dve"], e3)
            e4 = inc(tensor.matmul(pden[:, :], ones_row_f[:, :], mrow,
                                   start=True, stop=True), "a_pe")
            e5 = inc(tensor.matmul(pu2[0][:, :], ones_row_f[:, :], rrow,
                                   start=True, stop=True), "a_pe")
            scalar.wait_ge(sems["a_pe"], e5)
            scalar.copy(rBq_sb[:, :], pu2[0][:, :])
            e6 = inc(scalar.copy(rop[1][:, :], pden[:, :]), "a_act")
            guard("pden", "a_act", e6)
            guard(("pu2", 0), "a_act", e6)
            vector.wait_ge(sems["a_act"], e6)
            for dc in range(DC):
                sl = slice(dc * TB, (dc + 1) * TB)
                vector.tensor_sub(rop[0][:, :], axf[:, sl], rop[1][:, :])
                ins = vector.tensor_mul(axb[:, sl], rop[0][:, :], rBq_sb[:, :])
            xn3_done = inc(ins, "a_dve")
            gpsimd.wait_ge(sems["a_dve"], xn3_done)
            st3 = 0
            for c0 in range(0, DC, 4):
                st3 = dma("gpsimd", "a_gp",
                          xn3b[c0:c0 + 4, :, tb * TB:(tb + 1) * TB].rearrange(
                              "c p t -> p c t"),
                          axb[:, c0 * TB:(c0 + 4) * TB].rearrange("p (c t) -> p c t", c=4))
            guard("axf", "a_gp", st3)
            guard("axb", "a_gp", st3)

        p2_gp_total = Scnt["a_gp"]

        if STAGE <= 5:
            gpsimd.wait_ge(sems["a_gp"], Scnt["a_gp"])
            return Scnt

        sync.wait_ge(sems["a_gp"], p2_gp_total)
        gpsimd.wait_ge(sems["a_gp"], p2_gp_total)
        ffn_phase(3, NTBOWN)

        gpsimd.wait_ge(sems["f_gp"], Scnt["f_gp"])
        sync.wait_ge(sems["f_gp"], Scnt["f_gp"])
        return Scnt

    with nc.Block() as blk:

        @blk.sync
        def _(sync):
            walk("sync")

        @blk.tensor
        def _(tensor):
            walk("tensor")

        @blk.vector
        def _(vector):
            walk("vector")

        @blk.scalar
        def _(scalar):
            walk("scalar")

        @blk.gpsimd
        def _(gpsimd):
            walk("gpsimd")

    big.close()
    return nc



def _tile_w(w):
    din, dout = w.shape
    return np.ascontiguousarray(
        w.reshape(din // 128, 128, dout // 128, 128).transpose(2, 1, 0, 3)
        .reshape(dout // 128, 128, din // 128 * 128))


def _prep_inputs(inputs):
    x = np.asarray(inputs["L0_attn_hidden_states"], np.float32)
    W1 = np.asarray(inputs["L0_ffn_W1"], np.float32)
    Wg = np.asarray(inputs["L0_ffn_Wg"], np.float32)
    W2 = np.asarray(inputs["L0_ffn_W2"], np.float32)
    gate_w = np.asarray(inputs["L0_ffn_gate_w"], np.float32)

    lm = np.zeros((NR, NR), np.float32)
    for e in range(NR):
        for f in range(NR):
            lm[e, f] = 1.0 if f < e else 0.0
    oh = np.zeros((NR, NR, 128), np.float32)
    for e in range(NR):
        oh[:, e, :] = (np.arange(NR) == e).astype(np.float32)[:, None]

    common = {
        "w1t": np.stack([_tile_w(W1[e]) for e in range(NR)]),
        "wgt": np.stack([_tile_w(Wg[e]) for e in range(NR)]),
        "w2t": np.stack([_tile_w(W2[e]) for e in range(NR)]),
        "ws1t": _tile_w(np.asarray(inputs["L0_ffn_Ws1"], np.float32)),
        "wsgt": _tile_w(np.asarray(inputs["L0_ffn_Wsg"], np.float32)),
        "ws2t": _tile_w(np.asarray(inputs["L0_ffn_Ws2"], np.float32)),
        "wqt": _tile_w(np.asarray(inputs["L0_attn_w_q"], np.float32)),
        "wkt": _tile_w(np.asarray(inputs["L0_attn_w_k"], np.float32)),
        "wvt": _tile_w(np.asarray(inputs["L0_attn_w_v"], np.float32)),
        "wot": _tile_w(np.asarray(inputs["L0_attn_w_o"], np.float32)),
        "gwt": np.ascontiguousarray(
            gate_w.reshape(DC, 128, NR).transpose(1, 0, 2).reshape(128, DC * NR)),
        "gbt": np.asarray(inputs["L0_ffn_gate_b"], np.float32).reshape(NR, 1),
        "biast": np.asarray(inputs["L0_ffn_biases"], np.float32).reshape(1, NR),
        "lmt": lm.reshape(1, NR * NR),
        "oht": np.ascontiguousarray(oh.reshape(NR, NR * 128)),
        "idbt": np.eye(128, dtype=np.float32).astype(ml_dtypes.bfloat16),
        "idft": np.eye(128, dtype=np.float32),
    }

    inv_freq = 1.0 / (10000.0 ** (np.arange(0, HD, 2, dtype=np.float64) / HD))
    in_maps = []
    for c in range(NCORES):
        seq = c // 4
        s0 = (c % 4) * TOWN
        jj = np.arange(TEXT)
        pos_un = s0 - HALO + jj
        pos = np.mod(pos_un, S)
        xc = x[seq][pos]
        x0 = np.ascontiguousarray(xc.T.reshape(DC, 128, TEXT))
        freqs = pos.astype(np.float64)[None, :] * inv_freq[:, None]
        emb = np.concatenate([freqs, freqs], axis=0)
        mask = np.zeros((NTBOWN, 128, NKT * TB), np.float32)
        for qb in range(NTBOWN):
            qpos = s0 + qb * TB + np.arange(TB)
            for l in range(NKT):
                kk = np.arange(128)
                kpos = s0 - HALO + (qb * 4 + l) * 128 + kk
                dist = qpos[None, :] - kpos[:, None]
                ok = (dist >= 0) & (dist <= WIN) & (kpos[:, None] >= 0)
                mask[qb, :, l * TB:(l + 1) * TB] = ok.astype(np.float32)
        m = {
            "x0": x0,
            "cost": np.ascontiguousarray(np.cos(emb).astype(np.float32)),
            "sint": np.ascontiguousarray(np.sin(emb).astype(np.float32)),
            "maskt": mask.astype(ml_dtypes.bfloat16),
        }
        m.update(common)
        in_maps.append(m)
    return in_maps


def _run(inputs, dump=(), upto="all"):
    in_maps = _prep_inputs(inputs)
    nc = build_nc(dump=dump, upto=upto)
    return run_bass_kernel_spmd(nc, in_maps, core_ids=list(range(NCORES)))


def kernel(**inputs):
    res = _run(inputs)
    outs = [res.results[c]["out"] for c in range(NCORES)]
    full = np.stack([np.concatenate(outs[0:4], axis=0),
                     np.concatenate(outs[4:8], axis=0)], axis=0)
    return np.ascontiguousarray(full.astype(np.float32))


if __name__ == "__main__":
    nc = build_nc()
    print("build ok; instructions:", sum(1 for _ in nc.all_instructions()))


# version 05 (final)
# speedup vs baseline: 1.0706x; 1.0706x over previous
import math
from collections import defaultdict
from contextlib import ExitStack

import numpy as np
import ml_dtypes

import concourse.bass as bass
import concourse.mybir as mybir
from concourse.bass_utils import run_bass_kernel_spmd

F32 = mybir.dt.float32
BF16 = mybir.dt.bfloat16
AF = mybir.ActivationFunctionType
ALU = mybir.AluOpType
AX = mybir.AxisListType

B, S, D = 2, 8192, 2048
DC = D // 128
HC = 16
NH, NKV, HD = 16, 4, 128
WIN = 1024
NR = 7
EPS = 1e-5
TB = 512
TOWN = 2048
HALO = 1024
TEXT = TOWN + HALO
NTB = TEXT // TB
NTBOWN = TOWN // TB
NCORES = 8
SCL = 1.0 / math.sqrt(HD)
NWB = 8
NEXP = 8
NKT = 12

ENGS = ("sync", "tensor", "vector", "scalar", "gpsimd")


class _FakeInst:
    def then_inc(self, *a, **k):
        return self


_FAKE = _FakeInst()


class _Proxy:
    def __init__(self, real, active):
        self._real = real
        self._active = active

    def __getattr__(self, name):
        if self._active:
            return getattr(self._real, name)
        return lambda *a, **k: _FAKE


def build_nc(dump=(), upto="all"):
    STAGE = {"p1": 1, "a1": 2, "a2": 3, "c": 4, "d": 5, "all": 6}[upto]
    nc = bass.Bass("TRN2", target_bir_lowering=False, debug=False)

    X0 = nc.dram_tensor("x0", [DC, 128, TOWN], F32, kind="ExternalInput").ap()
    W1 = nc.dram_tensor("w1t", [NR, HC, 128, DC * 128], F32, kind="ExternalInput").ap()
    WG = nc.dram_tensor("wgt", [NR, HC, 128, DC * 128], F32, kind="ExternalInput").ap()
    W2 = nc.dram_tensor("w2t", [NR, DC, 128, HC * 128], F32, kind="ExternalInput").ap()
    WS1 = nc.dram_tensor("ws1t", [HC, 128, DC * 128], F32, kind="ExternalInput").ap()
    WSG = nc.dram_tensor("wsgt", [HC, 128, DC * 128], F32, kind="ExternalInput").ap()
    WS2 = nc.dram_tensor("ws2t", [DC, 128, HC * 128], F32, kind="ExternalInput").ap()
    WQ = nc.dram_tensor("wqt", [HC, 128, DC * 128], F32, kind="ExternalInput").ap()
    WK = nc.dram_tensor("wkt", [NKV, 128, DC * 128], F32, kind="ExternalInput").ap()
    WV = nc.dram_tensor("wvt", [NKV, 128, DC * 128], F32, kind="ExternalInput").ap()
    WO = nc.dram_tensor("wot", [DC, 128, HC * 128], F32, kind="ExternalInput").ap()
    GW = nc.dram_tensor("gwt", [128, DC * NR], F32, kind="ExternalInput").ap()
    GB = nc.dram_tensor("gbt", [NR, 1], F32, kind="ExternalInput").ap()
    BIAS = nc.dram_tensor("biast", [1, NR], F32, kind="ExternalInput").ap()
    LM = nc.dram_tensor("lmt", [1, NR * NR], F32, kind="ExternalInput").ap()
    OH = nc.dram_tensor("oht", [NR, NR * 128], F32, kind="ExternalInput").ap()
    COS = nc.dram_tensor("cost", [128, TEXT], F32, kind="ExternalInput").ap()
    SIN = nc.dram_tensor("sint", [128, TEXT], F32, kind="ExternalInput").ap()
    MASK = nc.dram_tensor("maskt", [NTBOWN, 128, NKT * TB], BF16, kind="ExternalInput").ap()
    IDB = nc.dram_tensor("idbt", [128, 128], BF16, kind="ExternalInput").ap()
    IDF = nc.dram_tensor("idft", [128, 128], F32, kind="ExternalInput").ap()
    HSEL = nc.dram_tensor("hselt", [128, NCORES], F32, kind="ExternalInput").ap()

    OUT = nc.dram_tensor("out", [TOWN, D], F32, kind="ExternalOutput").ap()

    def scratch(name, shape, dt):
        kind = "ExternalOutput" if name in dump else "Internal"
        return nc.dram_tensor(name, shape, dt, kind=kind).ap()

    w1b = scratch("w1b", [NR, HC, 128, DC * 128], BF16)
    wgb = scratch("wgb", [NR, HC, 128, DC * 128], BF16)
    w2b = scratch("w2b", [NR, DC, 128, HC * 128], BF16)
    ws1b = scratch("ws1b", [HC, 128, DC * 128], BF16)
    wsgb = scratch("wsgb", [HC, 128, DC * 128], BF16)
    ws2b = scratch("ws2b", [DC, 128, HC * 128], BF16)

    x1f = scratch("x1f", [DC, 128, TOWN], F32)
    xn2b = scratch("xn2b", [DC, 128, TOWN], BF16)
    xtail_t = nc.dram_tensor("xtail", [DC * 128, HALO], BF16)
    xgath_t = nc.dram_tensor("xgath", [NCORES * DC * 128, HALO], BF16)
    XT = xtail_t.ap().rearrange("(c p) t -> c p t", c=DC)
    XG = xgath_t.ap().rearrange("(r c p) t -> r c p t", r=NCORES, c=DC)
    x2f = scratch("x2f", [DC, 128, TOWN], F32)
    xn3b = scratch("xn3b", [DC, 128, TOWN], BF16)
    qtb = scratch("qtb", [NH, 128, TOWN], BF16)
    atb = scratch("atb", [HC, 128, TOWN], BF16)
    xn1d = scratch("xn1d", [DC, 128, TEXT], BF16) if "xn1d" in dump else None
    ktd = scratch("ktd", [128, NKV * TEXT], BF16) if "ktd" in dump else None
    vtd = scratch("vtd", [128, NKV * TEXT], BF16) if "ktd" in dump else None
    smxd = scratch("smxd", [128, NTB * 256], F32) if "smxd" in dump else None
    gtd = scratch("gtd", [128, NTB * 4 * NR], F32) if "smxd" in dump else None
    w7d = scratch("w7d", [NR, TEXT], F32) if "w7d" in dump else None
    wbd = scratch("wbd", [128, TEXT], F32) if "wbd" in dump else None

    CAST_MATS = []
    for e in range(NR):
        CAST_MATS.append((W1[e], w1b[e], HC))
        CAST_MATS.append((WG[e], wgb[e], HC))
        CAST_MATS.append((W2[e], w2b[e], DC))
    CAST_MATS.append((WS1, ws1b, HC))
    CAST_MATS.append((WSG, wsgb, HC))
    CAST_MATS.append((WS2, ws2b, DC))

    big = ExitStack()

    sems = {}
    for name in ["c_sq", "c_dve", "c_act", "c_gp", "cc",
                 "f_sq", "f_pe", "f_dve", "f_act", "f_gp",
                 "a_sq", "a_pe", "a_dve", "a_act", "a_gp"]:
        sems[name] = big.enter_context(nc.semaphore(name))

    def sb(name, shape, dt):
        return big.enter_context(nc.sbuf_tensor(name, shape, dt))

    ones_bf = sb("ones_bf", [128, 1], BF16)
    ones_row_f = sb("ones_row_f", [1, 128], F32)
    gb_sb = sb("gb_sb", [NR, 1], F32)
    bias_row = sb("bias_row", [1, NR], F32)
    lm_row = sb("lm_row", [1, NR * NR], F32)
    biases_bc = sb("biases_bc", [128, NR], F32)
    lm_bc = sb("lm_bc", [128, NR * NR], F32)
    oh_sb = sb("oh_sb", [NR, NR * 128], F32)
    gwb_sb = sb("gwb_sb", [128, DC * NR], BF16)
    idb_sb = sb("idb_sb", [128, 128], BF16)
    idf_sb = sb("idf_sb", [128, 128], F32)
    hsel_sb = sb("hsel_sb", [128, NCORES], F32)
    eps_sb = sb("eps_sb", [1, 1], F32)
    qn2_sb = sb("qn2_sb", [1, NH], F32)
    kn2_sb = sb("kn2_sb", [1, NKV], F32)
    negmb_sb = sb("negmb_sb", [128, NH], F32)
    rowtmp = sb("rowtmp", [1, 4 * TB], F32)

    ffn_sb = ExitStack()

    def fsb(name, shape, dt):
        return ffn_sb.enter_context(nc.sbuf_tensor(name, shape, dt))

    wb = [fsb(f"wb{i}", [128, 16 * 128], BF16) for i in range(NWB)]
    xf0 = fsb("xf0", [128, DC * TB], F32)
    xb_sb = fsb("xb_sb", [128, DC * TB], BF16)
    xn_sb = fsb("xn_sb", [128, DC * TB], BF16)
    acc_sb = fsb("acc_sb", [128, DC * TB], F32)
    a_sb = fsb("a_sb", [128, HC * TB], BF16)
    tsil = [fsb(f"tsil{i}", [128, TB], F32) for i in range(2)]
    t2_sb = fsb("t2_sb", [128, TB], F32)
    mB_sb = fsb("mB_sb", [128, TB], F32)
    rB_sb = fsb("rB_sb", [128, TB], F32)
    gt_sb = fsb("gt_sb", [128, 4 * NR], F32)
    w7_sb = fsb("w7_sb", [NR, TB], F32)
    smx_sb = fsb("smx_sb", [128, 256], F32)
    cmp_sb = fsb("cmp_sb", [128, 4 * NR * NR], F32)
    eq_sb = fsb("eq_sb", [128, 4 * NR * NR], F32)
    gwf_sb = fsb("gwf_sb", [128, DC * NR], F32)
    out_sb = [fsb(f"out_sb{i}", [128, D], F32) for i in range(2)]
    cf = [acc_sb[:, i * 2048:(i + 1) * 2048].bitcast(F32) for i in range(4)]

    ffn_ps = ExitStack()

    def fps(name):
        return ffn_ps.enter_context(nc.psum_tensor(name, [128, TB], F32))

    pA = [fps(f"pA{i}") for i in range(2)]
    pB = [fps(f"pB{i}") for i in range(2)]
    pW = [fps(f"pW{i}") for i in range(2)]
    pu = [fps(f"pu{i}") for i in range(2)]
    ffn_ps.close()
    ffn_sb.close()

    at_sb = ExitStack()

    def asb(name, shape, dt):
        return at_sb.enter_context(nc.sbuf_tensor(name, shape, dt))

    kt_sb = asb("kt_sb", [128, NKV * TEXT], BF16)
    vt_sb = asb("vt_sb", [128, NKV * TEXT], BF16)
    axf = asb("axf", [128, DC * TB], F32)
    axb = asb("axb", [128, DC * TB], BF16)
    acf = [asb(f"acf{i}", [128, 16 * 128], F32) for i in range(2)]
    wstr = [asb(f"wstr{i}", [128, 16 * 128], BF16) for i in range(4)]
    csb = [asb(f"csb{i}", [128, TB], F32) for i in range(2)]
    snb = [asb(f"snb{i}", [128, TB], F32) for i in range(2)]
    rop = [asb(f"rop{i}", [128, TB], F32) for i in range(2)]
    kvev = asb("kvev", [128, TB], BF16)
    pt_sb = asb("pt_sb", [128, NKT * TB], BF16)
    mk_sb = asb("mk_sb", [128, NKT * TB], BF16)
    qblk = [asb(f"qblk{i}", [128, TB], BF16) for i in range(2)]
    atev = asb("atev", [128, TB], BF16)
    rrow_sb = asb("rrow_sb", [1, TB], F32)
    rBq_sb = asb("rBq_sb", [128, TB], F32)
    sq_sb = asb("sq_sb", [128, TB], BF16)
    aob = asb("aob", [128, HC * TB], BF16)
    arow = rowtmp

    at_ps = ExitStack()
    pS = [at_ps.enter_context(nc.psum_tensor(f"pS{i}", [128, TB], F32)) for i in range(3)]
    pV = at_ps.enter_context(nc.psum_tensor("pV", [128, TB], F32))
    pden = at_ps.enter_context(nc.psum_tensor("pden", [128, TB], F32))
    pu2 = [at_ps.enter_context(nc.psum_tensor(f"pu2{i}", [128, TB], F32)) for i in range(2)]
    pbf = at_ps.enter_context(nc.psum_tensor("pbf", [128, 1024], BF16))
    at_ps.close()
    at_sb.close()

    def walk(active):
        Scnt = defaultdict(int)
        E = {n: _Proxy(getattr(nc, n), active == n) for n in ENGS}
        sync, tensor, vector, scalar, gpsimd = (E[n] for n in ENGS)
        guards = {}

        def inc(inst, sem, n=1):
            inst.then_inc(sems[sem], n)
            Scnt[sem] += n
            return Scnt[sem]

        def dma(q, sem, out, in_, **kw):
            return inc(E[q].dma_start(out=out, in_=in_, **kw), sem, 16)

        def guard(key, sem, cnt):
            guards[key] = (sem, cnt)

        def wait_guard(eng, key):
            if key in guards:
                s, c = guards[key]
                eng.wait_ge(sems[s], c)

        dma("gpsimd", "c_gp", gb_sb[:, :], GB)
        dma("gpsimd", "c_gp", bias_row[:, :], BIAS)
        dma("gpsimd", "c_gp", lm_row[:, :], LM)
        dma("gpsimd", "c_gp", oh_sb[:, :], OH)
        dma("gpsimd", "c_gp", gwf_sb[:, :], GW)
        dma("gpsimd", "c_gp", idb_sb[:, :], IDB)
        dma("gpsimd", "c_gp", hsel_sb[:, :], HSEL)
        c_small = dma("gpsimd", "c_gp", idf_sb[:, :], IDF)
        vector.wait_ge(sems["c_gp"], c_small)
        vector.memset(ones_bf[:, :], 1.0)
        vector.memset(eps_sb[:, :], EPS)
        vector.memset(ones_row_f[:, :], 1.0)
        vector.memset(qn2_sb[:, :], 0.0)
        vector.memset(kn2_sb[:, :], 0.0)
        cast_gw = inc(vector.tensor_copy(gwb_sb[:, :], gwf_sb[:, :]), "c_dve")
        tensor.wait_ge(sems["c_dve"], cast_gw)
        b2 = inc(tensor.matmul(pu[0][:, 0:NR], ones_row_f[:, :], bias_row[:, :],
                               start=True, stop=True), "f_pe")
        b2 = inc(tensor.matmul(pu[1][:, 0:NR * NR], ones_row_f[:, :], lm_row[:, :],
                               start=True, stop=True), "f_pe")
        scalar.wait_ge(sems["f_pe"], b2)
        scalar.copy(biases_bc[:, :], pu[0][:, 0:NR])
        bc_done = inc(scalar.copy(lm_bc[:, :], pu[1][:, 0:NR * NR]), "c_act")
        guard(("pu", 0), "c_act", bc_done)
        guard(("pu", 1), "c_act", bc_done)

        stripes = []
        for (src, dst, noc) in CAST_MATS:
            for oc in range(noc):
                stripes.append((src[oc], dst[oc]))
        dve_c = {}
        act_c = {}
        st_c = {}
        for i, (saps, daps) in enumerate(stripes):
            if i >= 4:
                sync.wait_ge(sems["c_gp"], st_c[i - 4])
            ldc = dma("sync", "c_sq", cf[i % 4][:, :], saps)
            if i % 2 == 0:
                vector.wait_ge(sems["c_sq"], ldc)
                if i >= 4:
                    vector.wait_ge(sems["c_gp"], st_c[i - 4])
                cc = inc(vector.tensor_copy(wb[i % 4][:, :], cf[i % 4][:, :]), "c_dve")
                dve_c[i] = cc
                gpsimd.wait_ge(sems["c_dve"], cc)
            else:
                scalar.wait_ge(sems["c_sq"], ldc)
                if i >= 4:
                    scalar.wait_ge(sems["c_gp"], st_c[i - 4])
                cc = inc(scalar.copy(wb[i % 4][:, :], cf[i % 4][:, :]), "c_act")
                act_c[i] = cc
                gpsimd.wait_ge(sems["c_act"], cc)
            st_c[i] = dma("gpsimd", "c_gp", daps, wb[i % 4][:, :])
        cast_store_total = Scnt["c_gp"]

        def ffn_phase(ph, ntb):
            stripe_i = [0]
            ring_pe = {}

            def stripe_load(mat_aps, oc):
                i = stripe_i[0]
                stripe_i[0] += 1
                bidx = i % NWB
                if ph == 1 and i < NWB:
                    sync.wait_ge(sems["c_gp"], cast_store_total)
                if i - NWB >= 0:
                    sync.wait_ge(sems["f_pe"], ring_pe[i - NWB])
                c = dma("sync", "f_sq", wb[bidx][:, :], mat_aps[oc])
                return i, bidx, c

            gp_xstore = {}
            last_xn_pe = [0]
            last_xb_pe = [0]
            prev_ep = [0]
            src = X0 if ph == 1 else x2f

            for tb in range(ntb):
                xbuf = xf0
                wait_guard(sync, "xf0")
                xl = 0
                for c0 in range(0, DC, 4):
                    xl = dma("sync", "f_sq",
                             xbuf[:, c0 * TB:(c0 + 4) * TB].rearrange("p (c t) -> p c t", c=4),
                             src[c0:c0 + 4, :, tb * TB:(tb + 1) * TB].rearrange("c p t -> p c t"))

                if ph == 1:
                    scalar.wait_ge(sems["f_sq"], xl)
                    scalar.wait_ge(sems["f_pe"], last_xb_pe[0])
                    for dc in range(DC):
                        ins = scalar.copy(xb_sb[:, dc * TB:(dc + 1) * TB],
                                          xbuf[:, dc * TB:(dc + 1) * TB])
                    xb_done = inc(ins, "f_act")
                    for dc in range(DC):
                        ins = scalar.activation(a_sb[:, dc * TB:(dc + 1) * TB],
                                                xb_sb[:, dc * TB:(dc + 1) * TB], AF.Square)
                    sq_done = inc(ins, "f_act")
                    tensor.wait_ge(sems["f_act"], xb_done)
                    wait_guard(tensor, ("pu", 0))
                    for dc in range(DC):
                        ins = tensor.matmul(pu[0][0:1, :], ones_bf[:, :],
                                            xb_sb[:, dc * TB:(dc + 1) * TB],
                                            start=(dc == 0), stop=(dc == DC - 1))
                    c1 = inc(ins, "f_pe")
                    tensor.wait_ge(sems["f_act"], sq_done)
                    wait_guard(tensor, ("pu", 1))
                    for dc in range(DC):
                        ins = tensor.matmul(pu[1][0:1, :], ones_bf[:, :],
                                            a_sb[:, dc * TB:(dc + 1) * TB],
                                            start=(dc == 0), stop=(dc == DC - 1))
                    c2 = inc(ins, "f_pe")
                    last_xb_pe[0] = c2
                    mrow = rowtmp[0:1, 0:TB]
                    trow = rowtmp[0:1, TB:2 * TB]
                    vrow = rowtmp[0:1, 2 * TB:3 * TB]
                    srow = rowtmp[0:1, 3 * TB:4 * TB]
                    vector.wait_ge(sems["f_pe"], c2)
                    vector.tensor_scalar_mul(mrow, pu[0][0:1, :], 1.0 / D)
                    c_ = inc(vector.tensor_scalar_mul(trow, pu[1][0:1, :], 1.0 / D), "f_dve")
                    vector.wait_ge(sems["f_dve"], c_)
                    c_ = inc(vector.tensor_mul(vrow, mrow, mrow), "f_dve")
                    vector.wait_ge(sems["f_dve"], c_)
                    r1 = inc(vector.tensor_sub(vrow, trow, vrow), "f_dve")
                    scalar.wait_ge(sems["f_dve"], r1)
                    r2 = inc(scalar.activation(srow, vrow, AF.Sqrt, bias=eps_sb[0:1, 0:1]), "f_act")
                    vector.wait_ge(sems["f_act"], r2)
                    r3 = inc(vector.reciprocal(vrow, srow), "f_dve")
                    tensor.wait_ge(sems["f_dve"], r3)
                    c3 = inc(tensor.matmul(pu[0][:, :], ones_row_f[:, :], mrow,
                                           start=True, stop=True), "f_pe")
                    c4 = inc(tensor.matmul(pu[1][:, :], ones_row_f[:, :], vrow,
                                           start=True, stop=True), "f_pe")
                    scalar.wait_ge(sems["f_pe"], c4)
                    scalar.copy(mB_sb[:, :], pu[0][:, :])
                    c5 = inc(scalar.copy(rB_sb[:, :], pu[1][:, :]), "f_act")
                    guard(("pu", 0), "f_act", c5)
                    guard(("pu", 1), "f_act", c5)
                    vector.wait_ge(sems["f_act"], c5)
                    vector.wait_ge(sems["f_pe"], last_xn_pe[0])
                    if tb >= 1:
                        vector.wait_ge(sems["f_gp"], gp_xstore[tb - 1])
                    vector.wait_ge(sems["f_sq"], xl)
                    for dc in range(DC):
                        sl = slice(dc * TB, (dc + 1) * TB)
                        vector.tensor_sub(t2_sb[:, :], xbuf[:, sl], mB_sb[:, :])
                        ins = vector.tensor_mul(xn_sb[:, sl], t2_sb[:, :], rB_sb[:, :])
                    xn_ready = ("f_dve", inc(ins, "f_dve"))
                    if xn1d is not None:
                        gpsimd.wait_ge(sems["f_dve"], xn_ready[1])
                        for c0 in range(0, DC, 4):
                            dma("gpsimd", "f_gp",
                                xn1d[c0:c0 + 4, :, tb * TB:(tb + 1) * TB].rearrange(
                                    "c p t -> p c t"),
                                xn_sb[:, c0 * TB:(c0 + 4) * TB].rearrange(
                                    "p (c t) -> p c t", c=4))
                else:
                    sync.wait_ge(sems["f_pe"], last_xn_pe[0])
                    ld = 0
                    for c0 in range(0, DC, 4):
                        ld = dma("sync", "f_sq",
                                 xn_sb[:, c0 * TB:(c0 + 4) * TB].rearrange(
                                     "p (c t) -> p c t", c=4),
                                 xn3b[c0:c0 + 4, :, tb * TB:(tb + 1) * TB].rearrange(
                                     "c p t -> p c t"))
                    xn_ready = ("f_sq", ld)

                tensor.wait_ge(sems[xn_ready[0]], xn_ready[1])
                if tb >= 1:
                    tensor.wait_ge(sems["f_dve"], prev_ep[0])
                for dc in range(DC):
                    ins = tensor.matmul(pW[0][0:NR, :],
                                        gwb_sb[:, dc * NR:(dc + 1) * NR],
                                        xn_sb[:, dc * TB:(dc + 1) * TB],
                                        start=(dc == 0), stop=(dc == DC - 1))
                g1 = inc(ins, "f_pe")
                vector.wait_ge(sems["f_pe"], g1)
                g2 = inc(vector.tensor_scalar(w7_sb[:, :], pW[0][0:NR, :],
                                              gb_sb[0:NR, 0:1], None, ALU.add), "f_dve")
                tensor.wait_ge(sems["f_dve"], g2)
                for t in range(4):
                    ins = tensor.transpose(pW[1][:, t * NR:(t + 1) * NR],
                                           w7_sb[0:NR, t * 128:(t + 1) * 128],
                                           idf_sb[0:NR, 0:NR])
                g3 = inc(ins, "f_pe")
                scalar.wait_ge(sems["f_pe"], g3)
                g4 = inc(scalar.copy(gt_sb[:, :], pW[1][:, 0:4 * NR]), "f_act")

                gt3 = gt_sb[:, :].rearrange("p (a f) -> p a f", a=4)
                nm = smx_sb[:, 0:4]
                nm3 = nm.rearrange("p (a f) -> p a f", f=1)
                z_ = smx_sb[:, 8:8 + 4 * NR]
                z3 = z_.rearrange("p (a f) -> p a f", a=4)
                pexp = smx_sb[:, 40:40 + 4 * NR]
                pex3 = pexp.rearrange("p (a f) -> p a f", a=4)
                ssum = smx_sb[:, 72:76]
                ss3 = ssum.rearrange("p (a f) -> p a f", f=1)
                prb = smx_sb[:, 80:80 + 4 * NR]
                prb3 = prb.rearrange("p (a f) -> p a f", a=4)
                bsd = smx_sb[:, 112:112 + 4 * NR]
                bsd3 = bsd.rearrange("p (a f) -> p a f", a=4)
                rnk = smx_sb[:, 144:144 + 4 * NR]
                rnk4 = rnk.rearrange("p (a e o) -> p a e o", a=4, o=1)
                sel = smx_sb[:, 176:176 + 4 * NR]
                wtk = smx_sb[:, 208:208 + 4 * NR]
                wt3 = wtk.rearrange("p (a f) -> p a f", a=4)
                wsum = smx_sb[:, 240:244]
                ws3 = wsum.rearrange("p (a f) -> p a f", f=1)
                cmp4 = cmp_sb[:, :].rearrange("p (a e f) -> p a e f", a=4, e=NR)
                eq4 = eq_sb[:, :].rearrange("p (a e f) -> p a e f", a=4, e=NR)
                a0 = bsd3.rearrange("p a (o f) -> p a o f", o=1).broadcast_to([128, 4, NR, NR])
                a1 = bsd3.rearrange("p a (f o) -> p a f o", o=1).broadcast_to([128, 4, NR, NR])

                rs_ = smx_sb[:, 4:8]
                rs3 = rs_.rearrange("p (a f) -> p a f", f=1)
                rws_ = smx_sb[:, 244:248]
                rws3 = rws_.rearrange("p (a f) -> p a f", f=1)

                def vstep(ins):
                    c = inc(ins, "f_dve")
                    vector.wait_ge(sems["f_dve"], c)
                    return c

                vector.wait_ge(sems["f_act"], g4)
                vstep(vector.reduce_max(nm3, gt3, AX.X, negate=True))
                s1 = vstep(vector.tensor_tensor(z3, gt3,
                                                nm3.broadcast_to([128, 4, NR]), ALU.add))
                scalar.wait_ge(sems["f_dve"], s1)
                s2 = inc(scalar.activation(pexp, z_, AF.Exp), "f_act")
                vector.wait_ge(sems["f_act"], s2)
                vstep(vector.reduce_sum(ss3, pex3, AX.X))
                vstep(vector.reciprocal(rs_, ssum))
                vstep(vector.tensor_tensor(prb3, pex3, rs3.broadcast_to([128, 4, NR]),
                                           ALU.mult))
                vstep(vector.tensor_tensor(
                    bsd3, prb3,
                    biases_bc[:, :].rearrange("p (o f) -> p o f", o=1)
                    .broadcast_to([128, 4, NR]), ALU.add))
                vstep(vector.tensor_tensor(cmp4, a0, a1, ALU.is_gt))
                vstep(vector.tensor_tensor(eq4, a0, a1, ALU.is_equal))
                vstep(vector.tensor_tensor(
                    eq4, eq4,
                    lm_bc[:, :].rearrange("p (o e f) -> p o e f", o=1, e=NR)
                    .broadcast_to([128, 4, NR, NR]), ALU.mult))
                vstep(vector.tensor_tensor(cmp4, cmp4, eq4, ALU.add))
                vstep(vector.reduce_sum(rnk4, cmp4, AX.X))
                vstep(vector.tensor_single_scalar(sel, rnk, 3.0, ALU.is_lt))
                vstep(vector.tensor_tensor(wtk, prb, sel, ALU.mult))
                vstep(vector.reduce_sum(ws3, wt3, AX.X))
                vstep(vector.reciprocal(rws_, wsum))
                s3 = vstep(vector.tensor_tensor(wt3, wt3,
                                                rws3.broadcast_to([128, 4, NR]),
                                                ALU.mult))
                if smxd is not None and ph == 1:
                    gpsimd.wait_ge(sems["f_dve"], s3)
                    dma("gpsimd", "f_gp", smxd[:, tb * 256:(tb + 1) * 256], smx_sb[:, :])
                    dma("gpsimd", "f_gp", gtd[:, tb * 4 * NR:(tb + 1) * 4 * NR],
                        gt_sb[:, :])
                tensor.wait_ge(sems["f_dve"], s3)
                for t in range(4):
                    ins = tensor.transpose(pW[0][0:NR, t * 128:(t + 1) * 128],
                                           wtk[:, t * NR:(t + 1) * NR], idf_sb[:, :])
                s4 = inc(ins, "f_pe")
                scalar.wait_ge(sems["f_pe"], s4)
                s5 = inc(scalar.copy(w7_sb[:, :], pW[0][0:NR, :]), "f_act")
                if w7d is not None and ph == 1:
                    gpsimd.wait_ge(sems["f_act"], s5)
                    dma("gpsimd", "f_gp", w7d[:, tb * TB:(tb + 1) * TB], w7_sb[:, :])

                mats = [(ws1b, wsgb, ws2b)] + [(w1b[e], wgb[e], w2b[e]) for e in range(NR)]
                silu_rec = {}
                a_rec = {}
                accrec = {}
                wb_pe = {}
                for e in range(NEXP):
                    m1, mg, m2 = mats[e]
                    if e >= 1:
                        tensor.wait_ge(sems["f_act"], s5)
                        wait_guard(tensor, ("puw", e % 2))
                        wb_pe[e] = inc(tensor.matmul(
                            pu[e % 2][:, :], oh_sb[:, (e - 1) * 128:e * 128],
                            w7_sb[:, :], start=True, stop=True), "f_pe")
                        if wbd is not None and ph == 1 and e == 1:
                            scalar.wait_ge(sems["f_pe"], wb_pe[e])
                            wdcp = inc(scalar.copy(rB_sb[:, :], pu[e % 2][:, :]), "f_act")
                            gpsimd.wait_ge(sems["f_act"], wdcp)
                            dma("gpsimd", "f_gp", wbd[:, tb * TB:(tb + 1) * TB],
                                rB_sb[:, :])
                    for hcb in range(HC):
                        i1, b1_, c1_ = stripe_load(m1, hcb)
                        ig, bg_, cg_ = stripe_load(mg, hcb)
                        ppa = pA[hcb % 2]
                        ppb = pB[hcb % 2]
                        tensor.wait_ge(sems["f_sq"], c1_)
                        wait_guard(tensor, ("pA", hcb % 2))
                        for dc in range(DC):
                            ins = tensor.matmul(ppa[:, :], wb[b1_][:, dc * 128:(dc + 1) * 128],
                                                xn_sb[:, dc * TB:(dc + 1) * TB],
                                                start=(dc == 0), stop=(dc == DC - 1))
                        mm1 = inc(ins, "f_pe")
                        ring_pe[i1] = mm1
                        tensor.wait_ge(sems["f_sq"], cg_)
                        wait_guard(tensor, ("pB", hcb % 2))
                        for dc in range(DC):
                            ins = tensor.matmul(ppb[:, :], wb[bg_][:, dc * 128:(dc + 1) * 128],
                                                xn_sb[:, dc * TB:(dc + 1) * TB],
                                                start=(dc == 0), stop=(dc == DC - 1))
                        mmg = inc(ins, "f_pe")
                        ring_pe[ig] = mmg
                        last_xn_pe[0] = mmg
                        scalar.wait_ge(sems["f_pe"], mm1)
                        wait_guard(scalar, ("tsil", hcb % 2))
                        srec = inc(scalar.activation(tsil[hcb % 2][:, :], ppa[:, :], AF.Silu),
                                   "f_act")
                        silu_rec[(e, hcb)] = srec
                        guard(("pA", hcb % 2), "f_act", srec)
                        vector.wait_ge(sems["f_act"], srec)
                        vector.wait_ge(sems["f_pe"], mmg)
                        asl = a_sb[:, hcb * TB:(hcb + 1) * TB]
                        if e == 0:
                            ins = vector.tensor_tensor(asl, tsil[hcb % 2][:, :], ppb[:, :],
                                                       ALU.mult)
                        else:
                            vector.wait_ge(sems["f_pe"], wb_pe[e])
                            vector.tensor_tensor(t2_sb[:, :], tsil[hcb % 2][:, :],
                                                 pu[e % 2][:, :], ALU.mult)
                            ins = vector.tensor_tensor(asl, t2_sb[:, :], ppb[:, :], ALU.mult)
                        arec = inc(ins, "f_dve")
                        a_rec[(e, hcb)] = arec
                        guard(("pB", hcb % 2), "f_dve", arec)
                        guard(("tsil", hcb % 2), "f_dve", arec)
                        if hcb == HC - 1:
                            guard(("puw", e % 2), "f_dve", arec)
                    for dc in range(DC):
                        i2, b2_, c2_ = stripe_load(m2, dc)
                        ppw = pW[dc % 2]
                        tensor.wait_ge(sems["f_sq"], c2_)
                        if dc == 0:
                            tensor.wait_ge(sems["f_dve"], a_rec[(e, HC - 1)])
                        wait_guard(tensor, ("pW", dc % 2))
                        for hcb in range(HC):
                            ins = tensor.matmul(ppw[:, :], wb[b2_][:, hcb * 128:(hcb + 1) * 128],
                                                a_sb[:, hcb * TB:(hcb + 1) * TB],
                                                start=(hcb == 0), stop=(hcb == HC - 1))
                        mmw = inc(ins, "f_pe")
                        ring_pe[i2] = mmw
                        vector.wait_ge(sems["f_pe"], mmw)
                        dsl = acc_sb[:, dc * TB:(dc + 1) * TB]
                        if e == 0:
                            ins = vector.tensor_copy(dsl, ppw[:, :])
                        else:
                            ins = vector.tensor_add(dsl, dsl, ppw[:, :])
                        acr = inc(ins, "f_dve")
                        accrec[(e, dc)] = acr
                        guard(("pW", dc % 2), "f_dve", acr)

                ep_ins = None
                for dc in range(DC):
                    sl = slice(dc * TB, (dc + 1) * TB)
                    vector.wait_ge(sems["f_dve"], accrec[(NEXP - 1, dc)])
                    ep_ins = vector.scalar_tensor_tensor(xbuf[:, sl], acc_sb[:, sl], 0.5,
                                                         xbuf[:, sl], ALU.mult, ALU.add)
                ep_all = inc(ep_ins, "f_dve")
                prev_ep[0] = ep_all
                if ph == 1:
                    gpsimd.wait_ge(sems["f_dve"], ep_all)
                    for c0 in range(0, DC, 4):
                        dma("gpsimd", "f_gp",
                            x1f[c0:c0 + 4, :, tb * TB:(tb + 1) * TB].rearrange(
                                "c p t -> p c t"),
                            xbuf[:, c0 * TB:(c0 + 4) * TB].rearrange("p (c t) -> p c t", c=4))
                    scalar.wait_ge(sems["f_dve"], ep_all)
                    scalar.wait_ge(sems["f_pe"], last_xb_pe[0])
                    for dc in range(DC):
                        ins = scalar.copy(xb_sb[:, dc * TB:(dc + 1) * TB],
                                          xbuf[:, dc * TB:(dc + 1) * TB])
                    xb1_done = inc(ins, "f_act")
                    for dc in range(DC):
                        ins = scalar.activation(a_sb[:, dc * TB:(dc + 1) * TB],
                                                xb_sb[:, dc * TB:(dc + 1) * TB], AF.Square)
                    sq1_done = inc(ins, "f_act")
                    tensor.wait_ge(sems["f_act"], xb1_done)
                    wait_guard(tensor, ("pu", 0))
                    for dc in range(DC):
                        ins = tensor.matmul(pu[0][0:1, :], ones_bf[:, :],
                                            xb_sb[:, dc * TB:(dc + 1) * TB],
                                            start=(dc == 0), stop=(dc == DC - 1))
                    d1 = inc(ins, "f_pe")
                    tensor.wait_ge(sems["f_act"], sq1_done)
                    wait_guard(tensor, ("pu", 1))
                    for dc in range(DC):
                        ins = tensor.matmul(pu[1][0:1, :], ones_bf[:, :],
                                            a_sb[:, dc * TB:(dc + 1) * TB],
                                            start=(dc == 0), stop=(dc == DC - 1))
                    d2 = inc(ins, "f_pe")
                    last_xb_pe[0] = d2
                    mrow = rowtmp[0:1, 0:TB]
                    rrow = rowtmp[0:1, TB:2 * TB]
                    trow = rowtmp[0:1, TB:2 * TB]
                    vrow = rowtmp[0:1, 2 * TB:3 * TB]
                    srow = rowtmp[0:1, 3 * TB:4 * TB]
                    vector.wait_ge(sems["f_pe"], d2)
                    vector.tensor_scalar_mul(mrow, pu[0][0:1, :], 1.0 / D)
                    c_ = inc(vector.tensor_scalar_mul(trow, pu[1][0:1, :], 1.0 / D), "f_dve")
                    vector.wait_ge(sems["f_dve"], c_)
                    c_ = inc(vector.tensor_mul(vrow, mrow, mrow), "f_dve")
                    vector.wait_ge(sems["f_dve"], c_)
                    e1 = inc(vector.tensor_sub(vrow, trow, vrow), "f_dve")
                    scalar.wait_ge(sems["f_dve"], e1)
                    e2 = inc(scalar.activation(srow, vrow, AF.Sqrt, bias=eps_sb[0:1, 0:1]), "f_act")
                    vector.wait_ge(sems["f_act"], e2)
                    e3 = inc(vector.reciprocal(rrow, srow), "f_dve")
                    tensor.wait_ge(sems["f_dve"], e3)
                    e4 = inc(tensor.matmul(pu[0][:, :], ones_row_f[:, :], mrow,
                                           start=True, stop=True), "f_pe")
                    e5 = inc(tensor.matmul(pu[1][:, :], ones_row_f[:, :], rrow,
                                           start=True, stop=True), "f_pe")
                    scalar.wait_ge(sems["f_pe"], e5)
                    scalar.copy(mB_sb[:, :], pu[0][:, :])
                    e6 = inc(scalar.copy(rB_sb[:, :], pu[1][:, :]), "f_act")
                    guard(("pu", 0), "f_act", e6)
                    guard(("pu", 1), "f_act", e6)
                    vector.wait_ge(sems["f_act"], e6)
                    vector.wait_ge(sems["f_pe"], last_xn_pe[0])
                    for dc in range(DC):
                        sl = slice(dc * TB, (dc + 1) * TB)
                        vector.tensor_sub(t2_sb[:, :], xbuf[:, sl], mB_sb[:, :])
                        ins = vector.tensor_mul(xn_sb[:, sl], t2_sb[:, :], rB_sb[:, :])
                    xn2_done = inc(ins, "f_dve")
                    last_xn_pe[0] = 0
                    gpsimd.wait_ge(sems["f_dve"], xn2_done)
                    st = 0
                    for c0 in range(0, DC, 4):
                        st = dma("gpsimd", "f_gp",
                                 xn2b[c0:c0 + 4, :, tb * TB:(tb + 1) * TB].rearrange(
                                     "c p t -> p c t"),
                                 xn_sb[:, c0 * TB:(c0 + 4) * TB].rearrange(
                                     "p (c t) -> p c t", c=4))
                    if tb >= 2:
                        for c0 in range(0, DC, 4):
                            st = dma("gpsimd", "f_gp",
                                     XT[c0:c0 + 4, :,
                                        (tb - 2) * TB:(tb - 1) * TB].rearrange(
                                         "c p t -> p c t"),
                                     xn_sb[:, c0 * TB:(c0 + 4) * TB].rearrange(
                                         "p (c t) -> p c t", c=4))
                    gp_xstore[tb] = st
                    guard("xf0", "f_gp", st)
                else:
                    tensor.wait_ge(sems["f_dve"], ep_all)
                    st = 0
                    tt = 0
                    for t in range(4):
                        ob = out_sb[t % 2]
                        wait_guard(scalar, ("out", t % 2))
                        for dc in range(DC):
                            pp = pu[dc % 2]
                            wait_guard(tensor, ("pu", dc % 2))
                            ins = tensor.transpose(
                                pp[:, (dc % 4) * 128:(dc % 4 + 1) * 128],
                                xbuf[:, dc * TB + t * 128:dc * TB + (t + 1) * 128],
                                idf_sb[:, :])
                            tt = inc(ins, "f_pe")
                            scalar.wait_ge(sems["f_pe"], tt)
                            ins2 = scalar.copy(
                                ob[:, dc * 128:(dc + 1) * 128],
                                pp[:, (dc % 4) * 128:(dc % 4 + 1) * 128])
                            td = inc(ins2, "f_act")
                            guard(("pu", dc % 2), "f_act", td)
                        gpsimd.wait_ge(sems["f_act"], td)
                        st = dma("gpsimd", "f_gp",
                                 OUT[(tb * 4 + t) * 128:(tb * 4 + t + 1) * 128, :],
                                 ob[:, :])
                        guard(("out", t % 2), "f_gp", st)
                    gp_xstore[tb] = st
                    guard("xf0", "f_pe", tt)
            return gp_xstore

        ffn_phase(1, NTBOWN)
        p1_gp_total = Scnt["f_gp"]

        gpsimd.wait_ge(sems["f_gp"], p1_gp_total)
        gpsimd.collective_compute(
            "AllGather", ALU.bypass,
            replica_groups=[list(range(NCORES))],
            ins=[xtail_t.ap().opt()],
            outs=[xgath_t.ap().opt()],
        ).then_inc(sems["cc"])
        Scnt["cc"] += 1
        gpsimd.wait_ge(sems["cc"], 1)

        if STAGE <= 1:
            gpsimd.wait_ge(sems["f_gp"], Scnt["f_gp"])
            return Scnt

        sync.wait_ge(sems["f_gp"], p1_gp_total)
        sync.wait_ge(sems["cc"], 1)

        def rope_apply(pp, dstap, cosap, sinap, rec_sem):
            vector.tensor_scalar_mul(rop[0][0:64, :], pp[64:128, :], -1.0)
            vector.tensor_copy(rop[0][64:128, :], pp[0:64, :])
            vector.tensor_tensor(rop[1][:, :], pp[:, :], cosap, ALU.mult)
            vector.tensor_tensor(rop[0][:, :], rop[0][:, :], sinap, ALU.mult)
            ins = vector.tensor_tensor(dstap, rop[1][:, :], rop[0][:, :], ALU.add)
            return inc(ins, rec_sem)

        halo_i = [0]
        halo_ld = {}
        halo_prev = [0]
        for tb in [2, 3, 4, 5, 0, 1]:
            csl = slice(tb * TB, (tb + 1) * TB)
            wait_guard(sync, ("csb", tb % 2))
            dma("sync", "a_sq", csb[tb % 2][:, :], COS[:, csl])
            cl = dma("sync", "a_sq", snb[tb % 2][:, :], SIN[:, csl])
            wait_guard(sync, "axb")
            if tb >= 2:
                for c0 in range(0, DC, 4):
                    xc_ = dma("sync", "a_sq",
                              axb[:, c0 * TB:(c0 + 4) * TB].rearrange(
                                  "p (c t) -> p c t", c=4),
                              xn2b[c0:c0 + 4, :,
                                   (tb - 2) * TB:(tb - 1) * TB].rearrange(
                                  "c p t -> p c t"))
                xl = ("a_sq", xc_)
            else:
                sync.wait_ge(sems["cc"], 1)
                tmpb = [axf[:, 0:1024].bitcast(BF16),
                        axf[:, 1024:2048].bitcast(BF16)]
                for c0 in range(0, DC, 4):
                    for r in range(NCORES):
                        i = halo_i[0]
                        halo_i[0] += 1
                        bidx = i % 2
                        if i >= 2:
                            sync.wait_ge(sems["a_dve"], halo_ld[i - 2])
                        c = dma("sync", "a_sq",
                                tmpb[bidx].rearrange("p (c t) -> p c t", c=4),
                                XG[r, c0:c0 + 4, :,
                                   tb * TB:(tb + 1) * TB].rearrange(
                                    "c p t -> p c t"))
                        vector.wait_ge(sems["a_sq"], c)
                        if i > 0:
                            vector.wait_ge(sems["a_dve"], halo_prev[0])
                        sl = axb[:, c0 * TB:(c0 + 4) * TB]
                        if r == 0:
                            if c0 == 0:
                                wait_guard(vector, "axb")
                            ins = vector.tensor_scalar(sl, tmpb[bidx],
                                                       hsel_sb[:, r:r + 1],
                                                       None, ALU.mult)
                        else:
                            ins = vector.scalar_tensor_tensor(
                                sl, tmpb[bidx], hsel_sb[:, r:r + 1], sl,
                                ALU.mult, ALU.add)
                        halo_prev[0] = inc(ins, "a_dve")
                        halo_ld[i] = halo_prev[0]
                xl = ("a_dve", halo_prev[0])
            for g in range(2 * NKV):
                srcw = WK[g] if g < NKV else WV[g - NKV]
                wait_guard(sync, ("acf", g % 2))
                c = dma("sync", "a_sq", acf[g % 2][:, :], srcw)
                vector.wait_ge(sems["a_sq"], c)
                wait_guard(vector, ("wstr", g % 4))
                wcast = inc(vector.tensor_copy(wstr[g % 4][:, :], acf[g % 2][:, :]),
                            "a_dve")
                guard(("acf", g % 2), "a_dve", wcast)
                pp = pS[g % 2]
                tensor.wait_ge(sems[xl[0]], xl[1])
                tensor.wait_ge(sems["a_dve"], wcast)
                wait_guard(tensor, ("pS", g % 2))
                for dc in range(DC):
                    ins = tensor.matmul(
                        pp[:, :], wstr[g % 4][:, dc * 128:(dc + 1) * 128],
                        axb[:, dc * TB:(dc + 1) * TB],
                        start=(dc == 0), stop=(dc == DC - 1))
                mmkv = inc(ins, "a_pe")
                guard("axb", "a_pe", mmkv)
                guard(("wstr", g % 4), "a_pe", mmkv)
                if g < NKV:
                    vector.wait_ge(sems["a_pe"], mmkv)
                    vector.wait_ge(sems["a_sq"], cl)
                    kdone = rope_apply(
                        pp, kt_sb[:, g * TEXT + tb * TB:g * TEXT + (tb + 1) * TB],
                        csb[tb % 2][:, :], snb[tb % 2][:, :], "a_dve")
                    guard(("pS", g % 2), "a_dve", kdone)
                    if g == NKV - 1:
                        guard(("csb", tb % 2), "a_dve", kdone)
                    scalar.wait_ge(sems["a_dve"], kdone)
                    wait_guard(scalar, "sq_sb")
                    ksq = inc(scalar.activation(
                        sq_sb[:, :],
                        kt_sb[:, g * TEXT + tb * TB:g * TEXT + (tb + 1) * TB],
                        AF.Square), "a_act")
                    tensor.wait_ge(sems["a_act"], ksq)
                    wait_guard(tensor, "pden")
                    nk1 = inc(tensor.matmul(pden[0:1, :], ones_bf[:, :], sq_sb[:, :],
                                            start=True, stop=True), "a_pe")
                    guard("sq_sb", "a_pe", nk1)
                    vector.wait_ge(sems["a_pe"], nk1)
                    c_ = inc(vector.reduce_max(arow[0:1, 0:1], pden[0:1, :], AX.X),
                             "a_dve")
                    vector.wait_ge(sems["a_dve"], c_)
                    nupd = inc(vector.tensor_max(kn2_sb[0:1, g:g + 1],
                                                 kn2_sb[0:1, g:g + 1],
                                                 arow[0:1, 0:1]), "a_dve")
                    vector.wait_ge(sems["a_dve"], nupd)
                    guard("pden", "a_dve", nupd)
                else:
                    gg = g - NKV
                    scalar.wait_ge(sems["a_pe"], mmkv)
                    wait_guard(scalar, "kvev")
                    vc = inc(scalar.copy(kvev[:, :], pp[:, :]), "a_act")
                    guard(("pS", g % 2), "a_act", vc)
                    tensor.wait_ge(sems["a_act"], vc)
                    wait_guard(tensor, "pbf")
                    for t in range(4):
                        ins = tensor.transpose(pbf[:, t * 128:(t + 1) * 128],
                                               kvev[:, t * 128:(t + 1) * 128],
                                               idb_sb[:, :])
                    vt_mm = inc(ins, "a_pe")
                    guard("kvev", "a_pe", vt_mm)
                    scalar.wait_ge(sems["a_pe"], vt_mm)
                    ins = scalar.copy(
                        vt_sb[:, gg * TEXT + tb * TB:gg * TEXT + (tb + 1) * TB],
                        pbf[:, 0:TB])
                    vtc = inc(ins, "a_act")
                    guard("pbf", "a_act", vtc)

        if ktd is not None:
            gpsimd.wait_ge(sems["a_dve"], Scnt["a_dve"])
            gpsimd.wait_ge(sems["a_act"], Scnt["a_act"])
            dma("gpsimd", "a_gp", ktd[:, :], kt_sb[:, :])
            dma("gpsimd", "a_gp", vtd[:, :], vt_sb[:, :])

        if STAGE <= 2:
            gpsimd.wait_ge(sems["a_gp"], Scnt["a_gp"])
            gpsimd.wait_ge(sems["a_dve"], Scnt["a_dve"])
            gpsimd.wait_ge(sems["a_act"], Scnt["a_act"])
            return Scnt

        qn_last = 0
        for tb in range(NTBOWN):
            tbe = tb + 2
            csl = slice(tbe * TB, (tbe + 1) * TB)
            wait_guard(sync, ("csb", tbe % 2))
            dma("sync", "a_sq", csb[tbe % 2][:, :], COS[:, csl])
            cl = dma("sync", "a_sq", snb[tbe % 2][:, :], SIN[:, csl])
            wait_guard(sync, "axb")
            xl = 0
            for c0 in range(0, DC, 4):
                xl = dma("sync", "a_sq",
                         axb[:, c0 * TB:(c0 + 4) * TB].rearrange("p (c t) -> p c t", c=4),
                         xn2b[c0:c0 + 4, :, tb * TB:(tb + 1) * TB].rearrange(
                             "c p t -> p c t"))
            for h in range(NH):
                wait_guard(sync, ("acf", h % 2))
                c = dma("sync", "a_sq", acf[h % 2][:, :], WQ[h])
                scalar.wait_ge(sems["a_sq"], c)
                wait_guard(scalar, ("wstr", h % 4))
                wcast = inc(scalar.copy(wstr[h % 4][:, :], acf[h % 2][:, :]), "a_act")
                guard(("acf", h % 2), "a_act", wcast)
                pp = pS[h % 2]
                tensor.wait_ge(sems["a_sq"], xl)
                tensor.wait_ge(sems["a_act"], wcast)
                wait_guard(tensor, ("pS", h % 2))
                for dc in range(DC):
                    ins = tensor.matmul(
                        pp[:, :], wstr[h % 4][:, dc * 128:(dc + 1) * 128],
                        axb[:, dc * TB:(dc + 1) * TB],
                        start=(dc == 0), stop=(dc == DC - 1))
                mmq = inc(ins, "a_pe")
                guard("axb", "a_pe", mmq)
                guard(("wstr", h % 4), "a_pe", mmq)
                vector.wait_ge(sems["a_pe"], mmq)
                vector.wait_ge(sems["a_sq"], cl)
                wait_guard(vector, ("qblk", h % 2))
                wait_guard(vector, ("qblk2", h % 2))
                qdone = rope_apply(pp, qblk[h % 2][:, :], csb[tbe % 2][:, :],
                                   snb[tbe % 2][:, :], "a_dve")
                guard(("pS", h % 2), "a_dve", qdone)
                if h == NH - 1:
                    guard(("csb", tbe % 2), "a_dve", qdone)
                gpsimd.wait_ge(sems["a_dve"], qdone)
                qst = dma("gpsimd", "a_gp", qtb[h, :, tb * TB:(tb + 1) * TB],
                          qblk[h % 2][:, :])
                guard(("qblk", h % 2), "a_gp", qst)
                scalar.wait_ge(sems["a_dve"], qdone)
                wait_guard(scalar, "sq_sb")
                qsq = inc(scalar.activation(sq_sb[:, :], qblk[h % 2][:, :], AF.Square),
                          "a_act")
                guard(("qblk2", h % 2), "a_act", qsq)
                tensor.wait_ge(sems["a_act"], qsq)
                wait_guard(tensor, "pden")
                nq1 = inc(tensor.matmul(pden[0:1, :], ones_bf[:, :], sq_sb[:, :],
                                        start=True, stop=True), "a_pe")
                guard("sq_sb", "a_pe", nq1)
                vector.wait_ge(sems["a_pe"], nq1)
                c_ = inc(vector.reduce_max(arow[0:1, 0:1], pden[0:1, :], AX.X), "a_dve")
                vector.wait_ge(sems["a_dve"], c_)
                qn_last = inc(vector.tensor_max(qn2_sb[0:1, h:h + 1],
                                                qn2_sb[0:1, h:h + 1],
                                                arow[0:1, 0:1]), "a_dve")
                vector.wait_ge(sems["a_dve"], qn_last)
                guard("pden", "a_dve", qn_last)
        a2_store_total = Scnt["a_gp"]

        scalar.wait_ge(sems["a_dve"], qn_last)
        inc(scalar.activation(arow[0:1, 0:NH], qn2_sb[0:1, :], AF.Sqrt), "a_act")
        n2 = inc(scalar.activation(arow[0:1, 32:32 + NKV], kn2_sb[0:1, :], AF.Sqrt),
                 "a_act")
        vector.wait_ge(sems["a_act"], n2)
        c_ = inc(vector.tensor_tensor(
            arow[0:1, 64:64 + NH].rearrange("o (g r) -> o g r", g=NKV),
            arow[0:1, 0:NH].rearrange("o (g r) -> o g r", g=NKV),
            arow[0:1, 32:32 + NKV].rearrange("o (g r) -> o g r", r=1)
            .broadcast_to([1, NKV, NKV]), ALU.mult), "a_dve")
        vector.wait_ge(sems["a_dve"], c_)
        n3 = inc(vector.tensor_scalar_mul(arow[0:1, 64:64 + NH],
                                          arow[0:1, 64:64 + NH], -SCL), "a_dve")
        tensor.wait_ge(sems["a_dve"], n3)
        wait_guard(tensor, "pden")
        n4 = inc(tensor.matmul(pden[:, 0:NH], ones_row_f[:, :], arow[0:1, 64:64 + NH],
                               start=True, stop=True), "a_pe")
        scalar.wait_ge(sems["a_pe"], n4)
        n5 = inc(scalar.copy(negmb_sb[:, :], pden[:, 0:NH]), "a_act")
        guard("pden", "a_act", n5)

        if STAGE <= 3:
            gpsimd.wait_ge(sems["a_gp"], Scnt["a_gp"])
            gpsimd.wait_ge(sems["a_act"], n5)
            return Scnt

        sync.wait_ge(sems["a_gp"], a2_store_total)
        sync.wait_ge(sems["a_act"], n5)
        for qb in range(NTBOWN):
            wait_guard(sync, "mk")
            ml = dma("sync", "a_sq", mk_sb[:, :], MASK[qb, :, :])
            for h in range(NH):
                g = h // 4
                wait_guard(sync, ("qblk", h % 2))
                wait_guard(sync, ("qblk2", h % 2))
                ql = dma("sync", "a_sq", qblk[h % 2][:, :],
                         qtb[h, :, qb * TB:(qb + 1) * TB])
                guard(("qblk2", h % 2), "a_sq", 0)
                guards.pop(("qblk2", h % 2), None)
                msk_last = 0
                for l in range(NKT):
                    pp = pS[l % 3]
                    ext_t = qb * 4 + l
                    tensor.wait_ge(sems["a_sq"], ql)
                    if qb == 0 and h == 0 and l == 0:
                        tensor.wait_ge(sems["a_act"], n5)
                    wait_guard(tensor, ("pS", l % 3))
                    ins = tensor.matmul(
                        pp[:, :], kt_sb[:, g * TEXT + ext_t * 128:g * TEXT + (ext_t + 1) * 128],
                        qblk[h % 2][:, :], start=True, stop=True)
                    mms = inc(ins, "a_pe")
                    if l == NKT - 1:
                        guard(("qblk", h % 2), "a_pe", mms)
                    scalar.wait_ge(sems["a_pe"], mms)
                    wait_guard(scalar, ("pt", l))
                    ins = scalar.activation(pt_sb[:, l * TB:(l + 1) * TB], pp[:, :], AF.Exp,
                                            bias=negmb_sb[:, h:h + 1], scale=SCL)
                    erec = inc(ins, "a_act")
                    guard(("pS", l % 3), "a_act", erec)
                    vector.wait_ge(sems["a_act"], erec)
                    vector.wait_ge(sems["a_sq"], ml)
                    ins = vector.tensor_tensor(pt_sb[:, l * TB:(l + 1) * TB],
                                               pt_sb[:, l * TB:(l + 1) * TB],
                                               mk_sb[:, l * TB:(l + 1) * TB],
                                               ALU.mult)
                    msk_last = inc(ins, "a_dve")
                guard("mk", "a_dve", msk_last)
                tensor.wait_ge(sems["a_dve"], msk_last)
                wait_guard(tensor, "pden")
                for l in range(NKT):
                    ins = tensor.matmul(pden[0:1, :], ones_bf[:, :],
                                        pt_sb[:, l * TB:(l + 1) * TB],
                                        start=(l == 0), stop=(l == NKT - 1))
                den_mm = inc(ins, "a_pe")
                wait_guard(tensor, "pV")
                for l in range(NKT):
                    ext_t = qb * 4 + l
                    ins = tensor.matmul(
                        pV[:, :], vt_sb[:, g * TEXT + ext_t * 128:g * TEXT + (ext_t + 1) * 128],
                        pt_sb[:, l * TB:(l + 1) * TB],
                        start=(l == 0), stop=(l == NKT - 1))
                pv_mm = inc(ins, "a_pe")
                for l in range(NKT):
                    guard(("pt", l), "a_pe", pv_mm)
                vector.wait_ge(sems["a_pe"], den_mm)
                rc = inc(vector.reciprocal(rrow_sb[:, :], pden[0:1, :]), "a_dve")
                guard("pden", "a_dve", rc)
                tensor.wait_ge(sems["a_dve"], rc)
                wait_guard(tensor, ("pu2", 0))
                rb1 = inc(tensor.matmul(pu2[0][:, :], ones_row_f[:, :], rrow_sb[:, :],
                                        start=True, stop=True), "a_pe")
                scalar.wait_ge(sems["a_pe"], rb1)
                rb2 = inc(scalar.copy(rBq_sb[:, :], pu2[0][:, :]), "a_act")
                guard(("pu2", 0), "a_act", rb2)
                vector.wait_ge(sems["a_act"], rb2)
                vector.wait_ge(sems["a_pe"], pv_mm)
                wait_guard(vector, "atev")
                ins = vector.tensor_tensor(atev[:, :], pV[:, :], rBq_sb[:, :], ALU.mult)
                at_done = inc(ins, "a_dve")
                guard("pV", "a_dve", at_done)
                gpsimd.wait_ge(sems["a_dve"], at_done)
                ast = dma("gpsimd", "a_gp", atb[h, :, qb * TB:(qb + 1) * TB], atev[:, :])
                guard("atev", "a_gp", ast)
        atb_store_total = Scnt["a_gp"]

        if STAGE <= 4:
            gpsimd.wait_ge(sems["a_gp"], Scnt["a_gp"])
            return Scnt

        sync.wait_ge(sems["a_gp"], atb_store_total)
        for tb in range(NTBOWN):
            wait_guard(sync, "aob")
            xl = 0
            for c0 in range(0, DC, 4):
                xl = dma("sync", "a_sq",
                         aob[:, c0 * TB:(c0 + 4) * TB].rearrange("p (c t) -> p c t", c=4),
                         atb[c0:c0 + 4, :, tb * TB:(tb + 1) * TB].rearrange(
                             "c p t -> p c t"))
            wait_guard(sync, "axf")
            x1l = 0
            for c0 in range(0, DC, 4):
                x1l = dma("sync", "a_sq",
                          axf[:, c0 * TB:(c0 + 4) * TB].rearrange("p (c t) -> p c t", c=4),
                          x1f[c0:c0 + 4, :, tb * TB:(tb + 1) * TB].rearrange(
                              "c p t -> p c t"))
            ev_rec = {}
            d_xn_guard = ("axb_d",)
            for dc in range(DC):
                wait_guard(sync, ("acf", dc % 2))
                c = dma("sync", "a_sq", acf[dc % 2][:, :], WO[dc])
                scalar.wait_ge(sems["a_sq"], c)
                wait_guard(scalar, ("wstr", dc % 4))
                wcast = inc(scalar.copy(wstr[dc % 4][:, :], acf[dc % 2][:, :]), "a_act")
                guard(("acf", dc % 2), "a_act", wcast)
                pp = pu2[dc % 2]
                tensor.wait_ge(sems["a_sq"], xl)
                tensor.wait_ge(sems["a_act"], wcast)
                wait_guard(tensor, ("pu2", dc % 2))
                for hcb in range(HC):
                    ins = tensor.matmul(
                        pp[:, :], wstr[dc % 4][:, hcb * 128:(hcb + 1) * 128],
                        aob[:, hcb * TB:(hcb + 1) * TB],
                        start=(hcb == 0), stop=(hcb == HC - 1))
                mmo = inc(ins, "a_pe")
                guard(("wstr", dc % 4), "a_pe", mmo)
                vector.wait_ge(sems["a_pe"], mmo)
                vector.wait_ge(sems["a_sq"], x1l)
                sl = slice(dc * TB, (dc + 1) * TB)
                ins = vector.tensor_add(axf[:, sl], axf[:, sl], pp[:, :])
                er = inc(ins, "a_dve")
                ev_rec[dc] = er
                guard(("pu2", dc % 2), "a_dve", er)
            x2_all = ev_rec[DC - 1]
            gpsimd.wait_ge(sems["a_dve"], x2_all)
            stx2 = 0
            for c0 in range(0, DC, 4):
                stx2 = dma("gpsimd", "a_gp",
                           x2f[c0:c0 + 4, :, tb * TB:(tb + 1) * TB].rearrange(
                               "c p t -> p c t"),
                           axf[:, c0 * TB:(c0 + 4) * TB].rearrange("p (c t) -> p c t", c=4))
            scalar.wait_ge(sems["a_dve"], x2_all)
            wait_guard(scalar, "axb")
            for dc in range(DC):
                ins = scalar.copy(axb[:, dc * TB:(dc + 1) * TB],
                                  axf[:, dc * TB:(dc + 1) * TB])
            xb3 = inc(ins, "a_act")
            for dc in range(DC):
                ins = scalar.activation(aob[:, dc * TB:(dc + 1) * TB],
                                        axb[:, dc * TB:(dc + 1) * TB], AF.Square)
            sq3 = inc(ins, "a_act")
            tensor.wait_ge(sems["a_act"], xb3)
            wait_guard(tensor, "pden")
            for dc in range(DC):
                ins = tensor.matmul(pden[0:1, :], ones_bf[:, :],
                                    axb[:, dc * TB:(dc + 1) * TB],
                                    start=(dc == 0), stop=(dc == DC - 1))
            d1 = inc(ins, "a_pe")
            tensor.wait_ge(sems["a_act"], sq3)
            wait_guard(tensor, ("pu2", 0))
            for dc in range(DC):
                ins = tensor.matmul(pu2[0][0:1, :], ones_bf[:, :],
                                    aob[:, dc * TB:(dc + 1) * TB],
                                    start=(dc == 0), stop=(dc == DC - 1))
            d2 = inc(ins, "a_pe")
            guard("aob", "a_pe", d2)
            mrow = arow[0:1, 0:TB]
            rrow = arow[0:1, TB:2 * TB]
            vrow = arow[0:1, 2 * TB:3 * TB]
            srow = arow[0:1, 3 * TB:4 * TB]
            vector.wait_ge(sems["a_pe"], d2)
            vector.tensor_scalar_mul(mrow, pden[0:1, :], 1.0 / D)
            c_ = inc(vector.tensor_scalar_mul(rrow, pu2[0][0:1, :], 1.0 / D), "a_dve")
            vector.wait_ge(sems["a_dve"], c_)
            c_ = inc(vector.tensor_mul(vrow, mrow, mrow), "a_dve")
            vector.wait_ge(sems["a_dve"], c_)
            e1 = inc(vector.tensor_sub(vrow, rrow, vrow), "a_dve")
            scalar.wait_ge(sems["a_dve"], e1)
            e2 = inc(scalar.activation(srow, vrow, AF.Sqrt, bias=eps_sb[0:1, 0:1]), "a_act")
            vector.wait_ge(sems["a_act"], e2)
            e3 = inc(vector.reciprocal(rrow, srow), "a_dve")
            tensor.wait_ge(sems["a_dve"], e3)
            e4 = inc(tensor.matmul(pden[:, :], ones_row_f[:, :], mrow,
                                   start=True, stop=True), "a_pe")
            e5 = inc(tensor.matmul(pu2[0][:, :], ones_row_f[:, :], rrow,
                                   start=True, stop=True), "a_pe")
            scalar.wait_ge(sems["a_pe"], e5)
            scalar.copy(rBq_sb[:, :], pu2[0][:, :])
            e6 = inc(scalar.copy(rop[1][:, :], pden[:, :]), "a_act")
            guard("pden", "a_act", e6)
            guard(("pu2", 0), "a_act", e6)
            vector.wait_ge(sems["a_act"], e6)
            for dc in range(DC):
                sl = slice(dc * TB, (dc + 1) * TB)
                vector.tensor_sub(rop[0][:, :], axf[:, sl], rop[1][:, :])
                ins = vector.tensor_mul(axb[:, sl], rop[0][:, :], rBq_sb[:, :])
            xn3_done = inc(ins, "a_dve")
            gpsimd.wait_ge(sems["a_dve"], xn3_done)
            st3 = 0
            for c0 in range(0, DC, 4):
                st3 = dma("gpsimd", "a_gp",
                          xn3b[c0:c0 + 4, :, tb * TB:(tb + 1) * TB].rearrange(
                              "c p t -> p c t"),
                          axb[:, c0 * TB:(c0 + 4) * TB].rearrange("p (c t) -> p c t", c=4))
            guard("axf", "a_gp", st3)
            guard("axb", "a_gp", st3)

        p2_gp_total = Scnt["a_gp"]

        if STAGE <= 5:
            gpsimd.wait_ge(sems["a_gp"], Scnt["a_gp"])
            return Scnt

        sync.wait_ge(sems["a_gp"], p2_gp_total)
        gpsimd.wait_ge(sems["a_gp"], p2_gp_total)
        ffn_phase(3, NTBOWN)

        gpsimd.wait_ge(sems["f_gp"], Scnt["f_gp"])
        sync.wait_ge(sems["f_gp"], Scnt["f_gp"])
        return Scnt

    with nc.Block() as blk:

        @blk.sync
        def _(sync):
            walk("sync")

        @blk.tensor
        def _(tensor):
            walk("tensor")

        @blk.vector
        def _(vector):
            walk("vector")

        @blk.scalar
        def _(scalar):
            walk("scalar")

        @blk.gpsimd
        def _(gpsimd):
            walk("gpsimd")

    big.close()
    return nc



def _tile_w(w):
    din, dout = w.shape
    return np.ascontiguousarray(
        w.reshape(din // 128, 128, dout // 128, 128).transpose(2, 1, 0, 3)
        .reshape(dout // 128, 128, din // 128 * 128))


def _prep_inputs(inputs):
    x = np.asarray(inputs["L0_attn_hidden_states"], np.float32)
    W1 = np.asarray(inputs["L0_ffn_W1"], np.float32)
    Wg = np.asarray(inputs["L0_ffn_Wg"], np.float32)
    W2 = np.asarray(inputs["L0_ffn_W2"], np.float32)
    gate_w = np.asarray(inputs["L0_ffn_gate_w"], np.float32)

    lm = np.zeros((NR, NR), np.float32)
    for e in range(NR):
        for f in range(NR):
            lm[e, f] = 1.0 if f < e else 0.0
    oh = np.zeros((NR, NR, 128), np.float32)
    for e in range(NR):
        oh[:, e, :] = (np.arange(NR) == e).astype(np.float32)[:, None]

    common = {
        "w1t": np.stack([_tile_w(W1[e]) for e in range(NR)]),
        "wgt": np.stack([_tile_w(Wg[e]) for e in range(NR)]),
        "w2t": np.stack([_tile_w(W2[e]) for e in range(NR)]),
        "ws1t": _tile_w(np.asarray(inputs["L0_ffn_Ws1"], np.float32)),
        "wsgt": _tile_w(np.asarray(inputs["L0_ffn_Wsg"], np.float32)),
        "ws2t": _tile_w(np.asarray(inputs["L0_ffn_Ws2"], np.float32)),
        "wqt": _tile_w(np.asarray(inputs["L0_attn_w_q"], np.float32)),
        "wkt": _tile_w(np.asarray(inputs["L0_attn_w_k"], np.float32)),
        "wvt": _tile_w(np.asarray(inputs["L0_attn_w_v"], np.float32)),
        "wot": _tile_w(np.asarray(inputs["L0_attn_w_o"], np.float32)),
        "gwt": np.ascontiguousarray(
            gate_w.reshape(DC, 128, NR).transpose(1, 0, 2).reshape(128, DC * NR)),
        "gbt": np.asarray(inputs["L0_ffn_gate_b"], np.float32).reshape(NR, 1),
        "biast": np.asarray(inputs["L0_ffn_biases"], np.float32).reshape(1, NR),
        "lmt": lm.reshape(1, NR * NR),
        "oht": np.ascontiguousarray(oh.reshape(NR, NR * 128)),
        "idbt": np.eye(128, dtype=np.float32).astype(ml_dtypes.bfloat16),
        "idft": np.eye(128, dtype=np.float32),
    }

    inv_freq = 1.0 / (10000.0 ** (np.arange(0, HD, 2, dtype=np.float64) / HD))
    in_maps = []
    for c in range(NCORES):
        seq = c // 4
        s0 = (c % 4) * TOWN
        jj = np.arange(TEXT)
        pos_un = s0 - HALO + jj
        pos = np.mod(pos_un, S)
        xc = x[seq][s0:s0 + TOWN]
        x0 = np.ascontiguousarray(xc.T.reshape(DC, 128, TOWN))
        freqs = pos.astype(np.float64)[None, :] * inv_freq[:, None]
        emb = np.concatenate([freqs, freqs], axis=0)
        mask = np.zeros((NTBOWN, 128, NKT * TB), np.float32)
        for qb in range(NTBOWN):
            qpos = s0 + qb * TB + np.arange(TB)
            for l in range(NKT):
                kk = np.arange(128)
                kpos = s0 - HALO + (qb * 4 + l) * 128 + kk
                dist = qpos[None, :] - kpos[:, None]
                ok = (dist >= 0) & (dist <= WIN) & (kpos[:, None] >= 0)
                mask[qb, :, l * TB:(l + 1) * TB] = ok.astype(np.float32)
        base = (c // 4) * 4
        srcr = base + (c - base - 1) % 4
        hsel = np.zeros((128, NCORES), np.float32)
        hsel[:, srcr] = 1.0
        m = {
            "x0": x0,
            "cost": np.ascontiguousarray(np.cos(emb).astype(np.float32)),
            "sint": np.ascontiguousarray(np.sin(emb).astype(np.float32)),
            "maskt": mask.astype(ml_dtypes.bfloat16),
            "hselt": hsel,
        }
        m.update(common)
        in_maps.append(m)
    return in_maps


def _run(inputs, dump=(), upto="all"):
    in_maps = _prep_inputs(inputs)
    nc = build_nc(dump=dump, upto=upto)
    return run_bass_kernel_spmd(nc, in_maps, core_ids=list(range(NCORES)))


def kernel(**inputs):
    res = _run(inputs)
    outs = [res.results[c]["out"] for c in range(NCORES)]
    full = np.stack([np.concatenate(outs[0:4], axis=0),
                     np.concatenate(outs[4:8], axis=0)], axis=0)
    return np.ascontiguousarray(full.astype(np.float32))


if __name__ == "__main__":
    nc = build_nc()
    print("build ok; instructions:", sum(1 for _ in nc.all_instructions()))
